# version 51
# baseline (speedup 1.0000x reference)
import numpy as np
import ml_dtypes
import concourse.bass as bass
import concourse.mybir as mybir
from concourse.bass_utils import run_bass_kernel_spmd

F32 = mybir.dt.float32
BF16 = mybir.dt.bfloat16
I32 = mybir.dt.int32
AF = mybir.ActivationFunctionType
ALU = mybir.AluOpType
AX = mybir.AxisListType.X

T = 4096
H = 1024
D = 2048
NS = 16
R = 64
II = 2816
TC = 512
CH = T // TC
DT = D // 128
HT = H // 128
IT = II // 128
NT = T // 128
QMAXI = 127.0


class Bld:

    ENG = ("v", "s", "t", "g", "y")

    def __init__(self, nc, sems, dpool):
        self.nc = nc
        self.sems = sems
        self.dpool = dpool
        self.dct = {q: 0 for q in dpool}
        self.cnt = {k: 0 for k in sems}
        self.wm = {e: {k: 0 for k in sems} for e in self.ENG}
        self.bufs = {}
        self.plan = {e: [] for e in self.ENG}

    def _need(self, reads, writes):
        need = {}

        def merge(d):
            for s, v in d.items():
                if v > need.get(s, 0):
                    need[s] = v

        for bk in reads:
            e = self.bufs.get(bk)
            if e:
                merge(e["w"])
        for bk in writes:
            e = self.bufs.get(bk)
            if e:
                merge(e["w"])
                merge(e["r"])
        return need

    def _waits(self, eng, need):
        wm = self.wm[eng]
        out = []
        for s, v in sorted(need.items()):
            if wm.get(s, 0) < v:
                out.append((s, v))
                wm[s] = v
        return out

    def _record(self, reads, writes, sem, val):
        for bk in writes:
            self.bufs[bk] = {"w": {sem: val}, "r": {}}
        for bk in reads:
            e = self.bufs.setdefault(bk, {"w": {}, "r": {}})
            if val > e["r"].get(sem, 0):
                e["r"][sem] = val

    def op(self, eng, fn, reads=(), writes=()):
        waits = self._waits(eng, self._need(reads, writes))
        self.cnt[eng] += 1
        val = self.cnt[eng]
        sems = self.sems

        def thunk(E, waits=tuple(waits), fn=fn, sem=eng):
            for s, v in waits:
                E.wait_ge(sems[s], v)
            fn(E).then_inc(sems[sem], 1)

        self.plan[eng].append(thunk)
        self._record(reads, writes, eng, val)

    def dma(self, fn, reads=(), writes=(), q="y"):
        pool = self.dpool[q]
        dsem = pool[self.dct[q] % len(pool)]
        self.dct[q] += 1
        waits = self._waits(q, self._need(reads, writes))
        self.cnt[dsem] += 16
        val = self.cnt[dsem]
        sems = self.sems

        def thunk(E, waits=tuple(waits), fn=fn, dsem=dsem):
            for s, v in waits:
                E.wait_ge(sems[s], v)
            fn(E).then_inc(sems[dsem], 16)

        self.plan[q].append(thunk)
        self._record(reads, writes, dsem, val)

    def fork(self, srcs, dsts):
        rec = {}
        for s in srcs:
            e = self.bufs.get(s)
            if e:
                for sem, v in list(e["w"].items()) + list(e["r"].items()):
                    if v > rec.get(sem, 0):
                        rec[sem] = v
        for d in dsts:
            old = self.bufs.get(d)
            neww = dict(rec)
            if old:
                for sem, v in list(old["w"].items()) + list(old["r"].items()):
                    if v > neww.get(sem, 0):
                        neww[sem] = v
            self.bufs[d] = {"w": neww, "r": {}}

    def raw_wait(self, eng, sem, val):
        sems = self.sems

        def thunk(E, sem=sem, val=val):
            E.wait_ge(sems[sem], val)

        self.plan[eng].append(thunk)
        if self.wm[eng].get(sem, 0) < val:
            self.wm[eng][sem] = val


def build_program():
    nc = bass.Bass("TRN2", target_bir_lowering=False)

    din = {}
    for name, shape in [
        ("x", [T, H]), ("Win", [H, 4 * H]), ("bin", [128, 32]),
        ("Wx", [128, DT * 96]), ("bx", [96]), ("Wdt", [R, D]), ("bdt", [128, DT]),
        ("A_log", [128, DT * NS]), ("Dp", [128, DT]), ("Wout", [D, H]),
        ("bout", [128, HT]),
        ("clip_x", [1]), ("clip_down", [1]),
        ("w_gate", [II, H]), ("w_up", [II, H]), ("w_down", [H, II]),
        ("ident_b", [128, 128]), ("ident_f", [128, 128]),
    ]:
        dt = BF16 if name == "ident_b" else F32
        din[name] = nc.dram_tensor(name, shape, dt, kind="ExternalInput").ap()
    dout = nc.dram_tensor("out", [T, H], F32, kind="ExternalOutput").ap()

    sc_xn = nc.dram_tensor("sc_xn", [T, H], F32).ap()
    sc_xnT = nc.dram_tensor("sc_xnT", [H, T], BF16).ap()
    sc_xqT = nc.dram_tensor("sc_xqT", [H, T], BF16).ap()
    sc_winb = nc.dram_tensor("sc_winb", [32, 128, HT, 128], BF16).ap()
    sc_woutb = nc.dram_tensor("sc_woutb", [HT, 128, DT, 128], BF16).ap()
    sc_wgT = nc.dram_tensor("sc_wgT", [IT, 128, HT, 128], BF16).ap()
    sc_wuT = nc.dram_tensor("sc_wuT", [IT, 128, HT, 128], BF16).ap()
    sc_wdT = nc.dram_tensor("sc_wdT", [HT, 128, IT, 128], BF16).ap()
    sc_xs = nc.dram_tensor("sc_xs", [D, T], F32).ap()
    sc_bcB = nc.dram_tensor("sc_bcB", [NS, T], BF16).ap()
    sc_bcC = nc.dram_tensor("sc_bcC", [NS, T], BF16).ap()
    sc_part = nc.dram_tensor("sc_part", [H, T], F32).ap()
    sc_mid = nc.dram_tensor("sc_mid", [II, T], F32).ap()
    sc_col = nc.dram_tensor("sc_col", [128], F32).ap()
    sc_wx = nc.dram_tensor("sc_wx", [128, DT * 96], BF16).ap()
    sc_s = nc.dram_tensor("sc_s", [8], F32).ap()
    cc1_in = nc.dram_tensor("cc1_in", [1, 1], F32).ap()
    cc1_out = nc.dram_tensor("cc1_out", [1, 1], F32).ap()
    cc2_in = nc.dram_tensor("cc2_in", [1, 1], F32).ap()
    cc2_out = nc.dram_tensor("cc2_out", [1, 1], F32).ap()

    import contextlib
    with contextlib.ExitStack() as _stack:
        def _sb(name, shape, dtype):
            return _stack.enter_context(nc.sbuf_tensor(name, shape, dtype))

        def _ps(name, shape, dtype):
            return _stack.enter_context(nc.psum_tensor(name, shape, dtype))

        def _sem(name):
            return _stack.enter_context(nc.semaphore(name))

        xs_a = _sb("xs_a", [128, 8192], F32)
        brep_a = _sb("brep_a", [128, 8192], BF16)
        h_a = _sb("h_a", [128, 8192], F32)
        z_a = _sb("z_a", [128, 8192], BF16)
        crep_a = _sb("crep_a", [128, 8192], BF16)
        ybf_a = _sb("ybf_a", [128, 8192], BF16)
        xnt = _sb("xnt", [128, 4096], BF16)
        wstr0 = _sb("wstr0", [128, 2048], BF16)
        wstr1 = _sb("wstr1", [128, 2048], BF16)
        delta0 = _sb("delta0", [128, TC], F32)
        delta1 = _sb("delta1", [128, TC], F32)
        gz0 = _sb("gz0", [128, TC], F32)
        gz1 = _sb("gz1", [128, TC], F32)
        dA0 = _sb("dA0", [128, TC], F32)
        dA4a = _sb("dA4a", [128, 4 * TC], F32)
        dA4b = _sb("dA4b", [128, 4 * TC], F32)
        zc = _sb("zc", [128, 1], F32)
        u0 = _sb("u0", [128, TC], F32)
        dxb = _sb("dxb", [128, TC], F32)
        stg0 = _sb("stg0", [128, TC], F32)
        stg1 = _sb("stg1", [128, TC], F32)
        xsb0 = _sb("xsb0", [128, TC], BF16)
        dpb = _sb("dpb", [R, TC], BF16)
        xp = _sb("xp", [96, TC], F32)
        cbf = _sb("cbf", [2 * NS, TC], BF16)
        A_sb = _sb("A_sb", [128, DT * NS], F32)
        carry = _sb("carry", [128, DT * NS], F32)
        bin_sb = _sb("bin_sb", [128, 32], F32)
        bdt_sb = _sb("bdt_sb", [128, DT], F32)
        dpv_sb = _sb("dpv_sb", [128, DT], F32)
        bout_sb = _sb("bout_sb", [128, HT], F32)
        bx_sb = _sb("bx_sb", [96, 1], F32)
        srg = _sb("srg", [128, IT], F32)
        sru = _sb("sru", [128, IT], F32)
        srd = _sb("srd", [128, HT], F32)
        sg = _sb("sg", [128, IT], F32)
        su = _sb("su", [128, IT], F32)
        sd = _sb("sd", [128, HT], F32)
        cols = _sb("cols", [128, 4], F32)
        lcol = _sb("lcol", [128, 8], F32)
        lcol2 = _sb("lcol2", [128, 8], F32)
        gcol = _sb("gcol", [128, 1], F32)
        gcol2 = _sb("gcol2", [128, 1], F32)
        tiny = _sb("tiny", [1, 16], F32)
        idb = _sb("idb", [128, 128], BF16)
        idf = _sb("idf", [128, 128], F32)
        wx2 = _sb("wx2", [128, 384], BF16)
        wdt_sb = _sb("wdt_sb", [R, D], BF16)
        cst = _sb("cst", [128, 2], F32)
        fga = _sb("fga", [128, TC], F32)
        fgb = _sb("fgb", [128, TC], F32)
        fga2 = _sb("fga2", [128, TC], F32)
        fgb2 = _sb("fgb2", [128, TC], F32)
        wfgt = _sb("wfgt", [128, 1024], BF16)
        wfut = _sb("wfut", [128, 1024], BF16)
        wfgt2 = _sb("wfgt2", [128, 1024], BF16)
        wfut2 = _sb("wfut2", [128, 1024], BF16)
        p0 = _ps("p0", [128, TC], F32)
        p1 = _ps("p1", [128, TC], F32)
        p2 = _ps("p2", [96, TC], F32)
        p3 = _ps("p3", [128, TC], F32)
        pA = _ps("pA", [128, TC], F32)
        pB = _ps("pB", [128, TC], F32)
        ptfA = _ps("ptfA", [128, 512], F32)
        ptfB = _ps("ptfB", [128, 128], F32)
        sem_v = _sem("sem_v")
        sem_s = _sem("sem_s")
        sem_t = _sem("sem_t")
        sem_g = _sem("sem_g")
        sem_y = _sem("sem_y")
        NDQ = 12
        dysems = {"dy%d" % i: _sem("sem_dy%d" % i) for i in range(NDQ)}
        dgsems = {"dg%d" % i: _sem("sem_dg%d" % i) for i in range(6)}
        dssems = {"ds%d" % i: _sem("sem_ds%d" % i) for i in range(4)}
        block = _stack.enter_context(nc.Block())

        sems = {"v": sem_v, "s": sem_s, "t": sem_t, "g": sem_g, "y": sem_y}
        sems.update(dysems)
        sems.update(dgsems)
        sems.update(dssems)
        b = Bld(nc, sems, {"y": sorted(dysems), "g": sorted(dgsems),
                           "s": sorted(dssems)})

        ppair = [p0, p1]
        wpair = [wstr0, wstr1]
        wkey = ["wstr0", "wstr1"]
        dA4p = [dA4a, dA4b]
        dA4keys = ["dA4_0", "dA4_1"]
        dA4v = [dA4a.rearrange("p (k t) -> p k t", k=4),
                dA4b.rearrange("p (k t) -> p k t", k=4)]
        dpair = [delta0, delta1]
        dkey = ["delta0", "delta1"]
        gzpair = [gz0, gz1]
        gzkey = ["gz0", "gz1"]
        stgpair = [stg0, stg1]
        stgkey = ["stg0", "stg1"]

        h_v = h_a.rearrange("p (n t) -> p n t", n=NS)
        h_red = h_a.rearrange("p (n t) -> p t n", n=NS)
        ptb_pair = [pA.bitcast(BF16)[:, 0:128], pB.bitcast(BF16)[:, 0:128]]
        xtile = h_a[:, 2048:3072]
        xsa_bf = xs_a.bitcast(BF16)
        xnt_t = [xnt.rearrange("p (k t) -> p k t", k=HT),
                 xsa_bf[:, 2048:6144].rearrange("p (k t) -> p k t", k=HT)]
        xnt_keys = ["xnt0", "xnt1"]
        ybf_t = [ybf_a.rearrange("p (d t) -> p d t", d=DT),
                 xsa_bf[:, 6144:14336].rearrange("p (d t) -> p d t", d=DT)]
        ybf_keys = ["ybf0", "ybf1"]
        xsld = [xs_a[:, 0:TC], xs_a[:, TC:2 * TC]]
        xsst = [xs_a[:, 7168:7680], xs_a[:, 7680:8192]]
        wl = [wstr0[:, 0:1024].rearrange("p (k f) -> p k f", k=HT),
              wstr1[:, 0:1024].rearrange("p (k f) -> p k f", k=HT)]
        zf32 = z_a.bitcast(F32)
        u4flat = zf32[:, 0:4 * TC]
        u4v = u4flat.rearrange("p (k t) -> p k t", k=4)
        ukeys = ["zu0", "zu1"]
        xqv = z_a[:, 4096:8192].rearrange("p (k t) -> p k t", k=HT)
        wpA = [wstr0[:, 0:1024].rearrange("p (k f) -> p k f", k=HT),
               wstr0[:, 1024:2048].rearrange("p (k f) -> p k f", k=HT),
               wstr1[:, 0:1024].rearrange("p (k f) -> p k f", k=HT),
               wstr1[:, 1024:2048].rearrange("p (k f) -> p k f", k=HT)]
        wpA_keys = ["wA0", "wA1", "wA2", "wA3"]
        wpB = [wfgt[:, :].rearrange("p (k f) -> p k f", k=HT),
               wfut[:, :].rearrange("p (k f) -> p k f", k=HT),
               wfgt2[:, :].rearrange("p (k f) -> p k f", k=HT),
               wfut2[:, :].rearrange("p (k f) -> p k f", k=HT)]
        wpB_keys = ["wB0", "wB1", "wB2", "wB3"]
        rotAB = {"A": 0, "B": 0}

        def grabA():
            r = rotAB["A"] % 4
            rotAB["A"] += 1
            return wpA[r], wpA_keys[r]

        def grabB():
            r = rotAB["B"] % 4
            rotAB["B"] += 1
            return wpB[r], wpB_keys[r]
        xs_v = xs_a.rearrange("p (d t) -> p d t", d=DT)
        brep_v = brep_a.rearrange("p (n t) -> p n t", n=NS)
        brep_f = brep_a.bitcast(F32)
        crep_v = crep_a.rearrange("p (n t) -> p n t", n=NS)
        ybf_v = ybf_a.rearrange("p (d t) -> p d t", d=DT)
        z_red = z_a.rearrange("p (n t) -> p t n", n=NS)
        i32_brep = brep_a.bitcast(I32)
        i32_h = h_a.bitcast(I32)
        i32_xs = xs_a.bitcast(I32)

        b.dma(lambda E: E.dma_start(out=A_sb[:, :], in_=din["A_log"][:, :]),
              writes=["A_sb"])
        b.op("s", lambda E: E.activation(out=h_a[:, 0:DT * NS], in_=A_sb[:, :], func=AF.Exp),
             reads=["A_sb"], writes=["h"])
        b.op("v", lambda E: E.tensor_scalar_mul(out=A_sb[:, :], in0=h_a[:, 0:DT * NS],
                                                scalar1=-1.0),
             reads=["h"], writes=["A_sb"])
        b.op("v", lambda E: E.memset(carry[:, :], 0.0), writes=["carry"])
        b.op("v", lambda E: E.memset(cst[:, 0:1], 1e-5), writes=["cst"])
        b.op("v", lambda E: E.memset(cst[:, 1:2], -1.0), writes=["cst"])
        b.op("v", lambda E: E.memset(gcol[:, :], 0.0), writes=["gcol"])
        b.op("v", lambda E: E.memset(zc[:, :], 0.0), writes=["zc"])
        b.op("v", lambda E: E.memset(gcol2[:, :], 0.0), writes=["gcol2"])
        b.dma(lambda E: E.dma_start(out=bin_sb[:, :], in_=din["bin"][:, :]),
              writes=["bin_sb"])
        b.dma(lambda E: E.dma_start(out=bdt_sb[:, :], in_=din["bdt"][:, :]),
              writes=["bdt_sb"])
        b.dma(lambda E: E.dma_start(out=dpv_sb[:, :], in_=din["Dp"][:, :]),
              writes=["dpv_sb"])
        b.dma(lambda E: E.dma_start(out=bout_sb[:, :], in_=din["bout"][:, :]),
              writes=["bout_sb"])
        b.dma(lambda E: E.dma_start(out=bx_sb[:, :], in_=din["bx"].unsqueeze(1)),
              writes=["bx_sb"])
        b.dma(lambda E: E.dma_start(out=idb[:, :], in_=din["ident_b"][:, :]), writes=["idb"])
        b.dma(lambda E: E.dma_start(out=idf[:, :], in_=din["ident_f"][:, :]), writes=["idf"])
        b.dma(lambda E: E.dma_start(out=tiny[0:1, 3:4], in_=din["clip_x"].unsqueeze(0)),
              writes=["tiny"])
        b.dma(lambda E: E.dma_start(out=tiny[0:1, 8:9], in_=din["clip_down"].unsqueeze(0)),
              writes=["tiny"])
        b.dma(lambda E: E.dma_start(out=h_a[:, 0:DT * 96], in_=din["Wx"][:, :]),
              writes=["h"])
        b.op("v", lambda E: E.tensor_copy(out=z_a[:, 0:DT * 96], in_=h_a[:, 0:DT * 96]),
             reads=["h"], writes=["z"])
        b.dma(lambda E: E.dma_start(out=sc_wx[:, :], in_=z_a[:, 0:DT * 96]),
              reads=["z"], writes=["sc_wx"], q="g")
        b.dma(lambda E: E.dma_start(out=h_a[0:R, 0:D], in_=din["Wdt"][:, :]), writes=["h"])
        b.op("v", lambda E: E.tensor_copy(out=wdt_sb[:, :], in_=h_a[0:R, 0:D]),
             reads=["h"], writes=["wdt_sb"])

        inv_h = 1.0 / float(H)
        p1sets = [
            (xtile, h_a[:, 0:H], h_a[:, H:2 * H], "h", lcol, "lcol",
             z_a[:, 0:H], "z", crep_a[:, 0:H], "crep"),
            (h_a[:, 5120:6144], h_a[:, 3072:4096], h_a[:, 4096:5120], "hB",
             lcol2, "lcolB", z_a[:, 2048:2048 + H], "zB",
             crep_a[:, 1024:1024 + H], "crepB"),
        ]
        for tt in range(NT):
            xt, sA, sB, hk2, lc, lck, zt, zk, crp, crk = p1sets[tt % 2]
            tsl = slice(tt * 128, (tt + 1) * 128)
            b.dma(lambda E, tsl=tsl, xt=xt: E.dma_start(out=xt, in_=din["x"][tsl, :]),
                  writes=[hk2])
            b.op("v", lambda E, lc=lc, xt=xt: E.reduce_sum(out=lc[:, 0:1], in_=xt, axis=AX),
                 reads=[hk2], writes=[lck])
            b.op("s", lambda E, sA=sA, xt=xt: E.activation(out=sA, in_=xt, func=AF.Square),
                 reads=[hk2], writes=[hk2])
            b.op("v", lambda E, lc=lc, sA=sA: E.reduce_sum(out=lc[:, 1:2], in_=sA, axis=AX),
                 reads=[hk2], writes=[lck])
            b.op("v", lambda E, lc=lc: E.tensor_scalar_mul(out=lc[:, 2:3], in0=lc[:, 0:1],
                                                           scalar1=inv_h),
                 reads=[lck], writes=[lck])
            b.op("v", lambda E, lc=lc: E.tensor_scalar_mul(out=lc[:, 3:4], in0=lc[:, 1:2],
                                                           scalar1=inv_h),
                 reads=[lck], writes=[lck])
            b.op("v", lambda E, lc=lc: E.scalar_tensor_tensor(out=lc[:, 4:5], in0=lc[:, 2:3],
                                                              scalar=lc[:, 2:3], in1=lc[:, 3:4],
                                                              op0=ALU.mult, op1=ALU.subtract),
                 reads=[lck], writes=[lck])
            b.op("s", lambda E, lc=lc: E.activation(out=lc[:, 5:6], in_=lc[:, 4:5],
                                                    func=AF.Sqrt,
                                                    scale=cst[:, 1:2], bias=cst[:, 0:1]),
                 reads=[lck, "cst"], writes=[lck])
            b.op("v", lambda E, lc=lc: E.reciprocal(out=lc[:, 6:7], in_=lc[:, 5:6]),
                 reads=[lck], writes=[lck])
            b.op("v", lambda E, lc=lc: E.scalar_tensor_tensor(out=lc[:, 7:8], in0=lc[:, 2:3],
                                                              scalar=-1.0, in1=lc[:, 6:7],
                                                              op0=ALU.mult, op1=ALU.mult),
                 reads=[lck], writes=[lck])
            b.op("s", lambda E, sB=sB, xt=xt, lc=lc: E.activation(
                out=sB, in_=xt, func=AF.Identity,
                scale=lc[:, 6:7], bias=lc[:, 7:8]),
                reads=[hk2, lck], writes=[hk2])
            b.op("v", lambda E, lc=lc, sB=sB: E.tensor_reduce(
                out=lc[:, 0:1], in_=sB, axis=AX,
                op=ALU.max, apply_absolute_value=True),
                reads=[hk2], writes=[lck])
            b.op("v", lambda E, lc=lc: E.tensor_max(out=gcol[:, :], in0=gcol[:, :],
                                                    in1=lc[:, 0:1]),
                 reads=[lck, "gcol"], writes=["gcol"])
            b.dma(lambda E, tsl=tsl, sB=sB: E.dma_start(out=sc_xn[tsl, :], in_=sB),
                  reads=[hk2], writes=["sc_xn.%d" % tt], q="g")
            b.op("v", lambda E, zt=zt, sB=sB: E.tensor_copy(out=zt, in_=sB),
                 reads=[hk2], writes=[zk])
            for k in range(HT):
                pb = ptb_pair[k % 2]
                pbk = "ptb%d" % (k % 2)
                b.op("t", lambda E, k=k, pb=pb, zt=zt: E.transpose(
                    pb[:, :], zt[:, k * 128:(k + 1) * 128], idb[:, :]),
                    reads=[zk, "idb"], writes=[pbk])
                b.op("s", lambda E, k=k, pb=pb, crp=crp: E.copy(
                    out=crp[:, k * 128:(k + 1) * 128], in_=pb[:, :]),
                    reads=[pbk], writes=[crk])
            b.dma(lambda E, tsl=tsl, crp=crp: E.dma_start(
                out=sc_xnT.rearrange("(k p) t -> p k t", p=128)[:, :, tsl],
                in_=crp[:, 0:H].rearrange("p (k f) -> p k f", k=HT)),
                reads=[crk], writes=["sc_xnT.%d" % tt], q="g")
        b.fork(["hB"], ["h"])
        b.fork(["lcolB"], ["lcol"])
        b.fork(["zB"], ["z"])
        b.fork(["crepB"], ["crep"])

        tinyrow = fga[0:1, 0:128]

        def allreduce_scalar(colbuf, colkey, cc_in, cc_out, clip_slot, s_slot, cc_name, sc_off):
            b.fork(["fga0"], ["tinyrow"])
            b.dma(lambda E: E.dma_start(out=sc_col.rearrange("(p o) -> p o", o=1),
                                        in_=colbuf[:, :]),
                  reads=[colkey], writes=["sc_col" + colkey])
            b.dma(lambda E: E.dma_start(out=tinyrow[:, :], in_=sc_col.unsqueeze(0)),
                  reads=["sc_col" + colkey], writes=["tinyrow"])
            b.op("v", lambda E: E.tensor_reduce(out=tiny[0:1, 0:1], in_=tinyrow[:, :],
                                                axis=AX, op=ALU.max),
                 reads=["tinyrow"], writes=["tiny"])
            b.dma(lambda E: E.dma_start(out=cc_in[:, :], in_=tiny[0:1, 0:1]),
                  reads=["tiny"], writes=[cc_name + "_in"])
            b.op("g", lambda E: E.collective_compute(
                "AllReduce", ALU.max, ins=[cc_in[:, :]], outs=[cc_out[:, :]],
                replica_groups=[[0, 1, 2, 3, 4, 5, 6, 7]]),
                reads=[cc_name + "_in"], writes=[cc_name + "_out"])
            b.dma(lambda E: E.dma_start(out=tiny[0:1, 1:2], in_=cc_out[:, :]),
                  reads=[cc_name + "_out"], writes=["tiny"])
            b.op("s", lambda E: E.activation(out=tiny[0:1, 4:5],
                                             in_=tiny[0:1, clip_slot:clip_slot + 1],
                                             func=AF.Sigmoid),
                 reads=["tiny"], writes=["tiny"])
            b.op("v", lambda E: E.tensor_mul(out=tiny[0:1, 5:6], in0=tiny[0:1, 4:5],
                                             in1=tiny[0:1, 1:2]),
                 reads=["tiny"], writes=["tiny"])
            b.op("v", lambda E: E.tensor_scalar_mul(out=tiny[0:1, 6:7], in0=tiny[0:1, 5:6],
                                                    scalar1=1.0 / QMAXI),
                 reads=["tiny"], writes=["tiny"])
            b.op("v", lambda E: E.reciprocal(out=tiny[0:1, 7:8], in_=tiny[0:1, 6:7]),
                 reads=["tiny"], writes=["tiny"])
            b.dma(lambda E: E.dma_start(out=sc_s.unsqueeze(0)[:, sc_off:sc_off + 2],
                                        in_=tiny[0:1, 6:8]),
                  reads=["tiny"], writes=["sc_s" + cc_name])
            b.dma(lambda E: E.dma_start(
                out=cols[:, s_slot:s_slot + 2],
                in_=sc_s.unsqueeze(0)[:, sc_off:sc_off + 2].broadcast_to([128, 2])),
                reads=["sc_s" + cc_name], writes=["cols"])

        allreduce_scalar(gcol, "gcol", cc1_in, cc1_out, 3, 0, "cc1", 0)

        def wquant(src, ntile, C, srow_sb, srowkey, dstT, dstkey):
            KB = C // 128
            for r in range(ntile):
                rsl = slice(r * 128, (r + 1) * 128)
                b.dma(lambda E, rsl=rsl, C=C, src=src: E.dma_start(out=xs_a[:, 0:C],
                                                                   in_=src[rsl, :]),
                      writes=["xs"])
                b.op("v", lambda E, C=C: E.tensor_reduce(out=lcol[:, 0:1], in_=xs_a[:, 0:C],
                                                         axis=AX, op=ALU.max,
                                                         apply_absolute_value=True),
                     reads=["xs"], writes=["lcol"])
                b.op("v", lambda E: E.reciprocal(out=lcol[:, 1:2], in_=lcol[:, 0:1]),
                     reads=["lcol"], writes=["lcol"])
                b.op("v", lambda E, r=r, srow_sb=srow_sb: E.tensor_scalar_mul(
                    out=srow_sb[:, r:r + 1], in0=lcol[:, 0:1], scalar1=1.0 / QMAXI),
                    reads=["lcol"], writes=[srowkey])
                b.op("v", lambda E: E.tensor_scalar_mul(out=lcol[:, 2:3], in0=lcol[:, 1:2],
                                                        scalar1=QMAXI),
                     reads=["lcol"], writes=["lcol"])
                b.op("s", lambda E, C=C: E.activation(out=h_a[:, 0:C], in_=xs_a[:, 0:C],
                                                      func=AF.Identity, scale=lcol[:, 2:3]),
                     reads=["xs", "lcol"], writes=["h"])
                b.op("v", lambda E, C=C: E.tensor_copy(out=i32_brep[:, 0:C], in_=h_a[:, 0:C]),
                     reads=["h"], writes=["brep"])
                b.op("v", lambda E, C=C: E.tensor_copy(out=z_a[:, 0:C], in_=i32_brep[:, 0:C]),
                     reads=["brep"], writes=["z"])
                for k in range(KB):
                    pb = ptb_pair[k % 2]
                    pbk = "ptb%d" % (k % 2)
                    b.op("t", lambda E, k=k, pb=pb: E.transpose(
                        pb[:, :], z_a[:, k * 128:(k + 1) * 128], idb[:, :]),
                        reads=["z", "idb"], writes=[pbk])
                    b.op("s", lambda E, k=k, pb=pb: E.copy(
                        out=crep_a[:, k * 128:(k + 1) * 128], in_=pb[:, :]),
                        reads=[pbk], writes=["crep"])
                b.dma(lambda E, r=r, KB=KB, C=C, dstT=dstT: E.dma_start(
                    out=dstT[r, :, :, :],
                    in_=crep_a[:, 0:C].rearrange("p (k f) -> p k f", k=KB)),
                    reads=["crep"], writes=[dstkey], q="g")

        wquant(din["w_gate"], IT, H, srg, "srg", sc_wgT, "sc_wgT")
        wquant(din["w_up"], IT, H, sru, "sru", sc_wuT, "sc_wuT")
        wquant(din["w_down"], HT, II, srd, "srd", sc_wdT, "sc_wdT")

        for r in range(HT):
            rsl = slice(r * 128, (r + 1) * 128)
            b.dma(lambda E, rsl=rsl: E.dma_start(out=xs_a[:, 0:4096], in_=din["Win"][rsl, :]),
                  writes=["xs"])
            b.op("v", lambda E: E.tensor_copy(out=z_a[:, 0:4096], in_=xs_a[:, 0:4096]),
                 reads=["xs"], writes=["z"])
            b.dma(lambda E, r=r: E.dma_start(
                out=sc_winb[:, :, r, :].transpose([1, 0, 2]),
                in_=z_a[:, 0:4096].rearrange("p (ct c) -> p ct c", ct=32)),
                reads=["z"], writes=["sc_winb"], q="g")
        for r in range(DT):
            rsl = slice(r * 128, (r + 1) * 128)
            b.dma(lambda E, rsl=rsl: E.dma_start(out=xs_a[:, 0:H], in_=din["Wout"][rsl, :]),
                  writes=["xs"])
            b.op("v", lambda E: E.tensor_copy(out=z_a[:, 0:H], in_=xs_a[:, 0:H]),
                 reads=["xs"], writes=["z"])
            b.dma(lambda E, r=r: E.dma_start(
                out=sc_woutb[:, :, r, :].transpose([1, 0, 2]),
                in_=z_a[:, 0:H].rearrange("p (ht c) -> p ht c", ht=HT)),
                reads=["z"], writes=["sc_woutb"], q="g")

        b.op("v", lambda E: E.tensor_scalar(out=sg[:, :], in0=srg[:, :], scalar1=cols[:, 0:1],
                                            scalar2=None, op0=ALU.mult),
             reads=["srg", "cols"], writes=["sg"])
        b.op("v", lambda E: E.tensor_scalar(out=su[:, :], in0=sru[:, :], scalar1=cols[:, 0:1],
                                            scalar2=None, op0=ALU.mult),
             reads=["sru", "cols"], writes=["su"])

        b.fork(["brep"], ["brepB"])
        b.fork(["z"], ["zB"])
        b.fork(["crep"], ["crepB"])
        b.fork(["h"], ["hB"])
        p1csets = [
            (xtile, "h", brep_f[:, 0:H], "brep", i32_h[:, 0:H],
             z_a[:, 0:H], "z", crep_a[:, 0:H], "crep"),
            (h_a[:, 5120:6144], "hB", brep_f[:, 1024:2048], "brepB",
             i32_h[:, 3072:3072 + H], z_a[:, 2048:2048 + H], "zB",
             crep_a[:, 1024:1024 + H], "crepB"),
        ]
        for tt in range(NT):
            xt, hk2, brf, brk, i32t, zt, zk, crp, crk = p1csets[tt % 2]
            tsl = slice(tt * 128, (tt + 1) * 128)
            b.dma(lambda E, tsl=tsl, tt=tt, xt=xt: E.dma_start(out=xt, in_=sc_xn[tsl, :]),
                  reads=["sc_xn.%d" % tt], writes=[hk2])
            b.op("s", lambda E, brf=brf, xt=xt: E.activation(
                out=brf, in_=xt, func=AF.Identity, scale=cols[:, 1:2]),
                reads=[hk2, "cols"], writes=[brk])
            b.op("v", lambda E, i32t=i32t, brf=brf: E.tensor_copy(out=i32t, in_=brf),
                 reads=[brk], writes=[hk2])
            b.op("v", lambda E, zt=zt, i32t=i32t: E.tensor_scalar(
                out=zt, in0=i32t,
                scalar1=-(QMAXI + 1.0), scalar2=QMAXI,
                op0=ALU.max, op1=ALU.min),
                reads=[hk2], writes=[zk])
            for k in range(HT):
                pb = ptb_pair[k % 2]
                pbk = "ptb%d" % (k % 2)
                b.op("t", lambda E, k=k, pb=pb, zt=zt: E.transpose(
                    pb[:, :], zt[:, k * 128:(k + 1) * 128], idb[:, :]),
                    reads=[zk, "idb"], writes=[pbk])
                b.op("s", lambda E, k=k, pb=pb, crp=crp: E.copy(
                    out=crp[:, k * 128:(k + 1) * 128], in_=pb[:, :]),
                    reads=[pbk], writes=[crk])
            b.dma(lambda E, tsl=tsl, crp=crp: E.dma_start(
                out=sc_xqT.rearrange("(k p) t -> p k t", p=128)[:, :, tsl],
                in_=crp[:, 0:H].rearrange("p (k f) -> p k f", k=HT)),
                reads=[crk], writes=["sc_xqT.%d" % tt], q="g")
        b.fork(["brepB"], ["brep"])
        b.fork(["zB"], ["z"])
        b.fork(["crepB"], ["crep"])
        b.fork(["hB"], ["h"])

        F1_STEPS = CH * (IT + 1)
        f1L = {"i": 0}
        f1q = []

        def f1_load():
            i = f1L["i"]
            if i >= F1_STEPS:
                return
            fc = i // (IT + 1)
            st = i % (IT + 1)
            fsl = slice(fc * TC, (fc + 1) * TC)
            if st == 0:
                if f1q:
                    return
                f1L["i"] += 1
                tts = ["sc_xqT.%d" % (4 * fc + j) for j in range(4)]
                b.dma(lambda E, fsl=fsl: E.dma_start(
                    out=xqv[:, :, :],
                    in_=sc_xqT.rearrange("(k p) t -> p k t", p=128)[:, :, fsl]),
                    reads=tts, writes=["zxq"], q="s")
                return
            f1L["i"] += 1
            it = st - 1
            wfg, gk = grabB()
            wfu, uk2 = grabB()
            b.dma(lambda E, it=it, wfg=wfg: E.dma_start(out=wfg[:, :, :],
                                                        in_=sc_wgT[it, :, :, :]),
                  reads=["sc_wgT"], writes=[gk], q="s")
            b.dma(lambda E, it=it, wfu=wfu: E.dma_start(out=wfu[:, :, :],
                                                        in_=sc_wuT[it, :, :, :]),
                  reads=["sc_wuT"], writes=[uk2], q="s")
            f1q.append((it, fsl, fc, wfg, gk, wfu, uk2))

        fgap = [fga, fgb, fga2, fgb2]
        f1cnt = {"i": 0}

        def f1_mm():
            if not f1q:
                return
            it, fsl, fc, wfg, gk, wfu, uk2 = f1q.pop(0)
            fp2 = f1cnt["i"] % 2
            f1cnt["i"] += 1
            ga, gb = fgap[2 * fp2], fgap[2 * fp2 + 1]
            gak, gbk = "fga%d" % fp2, "fgb%d" % fp2
            ppg, ppgk = cl_bank()
            ppu, ppuk = cl_bank()
            for hk in range(HT):
                b.op("t", lambda E, hk=hk, ppg=ppg, wfg=wfg: E.matmul(
                    ppg[:, :], wfg[:, hk, :], xqv[:, hk, :],
                    start=(hk == 0), stop=(hk == HT - 1)),
                    reads=[gk, "zxq"], writes=[ppgk])
            for hk in range(HT):
                b.op("t", lambda E, hk=hk, ppu=ppu, wfu=wfu: E.matmul(
                    ppu[:, :], wfu[:, hk, :], xqv[:, hk, :],
                    start=(hk == 0), stop=(hk == HT - 1)),
                    reads=[uk2, "zxq"], writes=[ppuk])

            def tail(it=it, fsl=fsl, fc=fc, ppg=ppg, ppgk=ppgk, ppu=ppu, ppuk=ppuk,
                     ga=ga, gb=gb, gak=gak, gbk=gbk):
                b.op("s", lambda E: E.activation(out=ga[:, :], in_=ppg[:, :],
                                                 func=AF.Silu, scale=sg[:, it:it + 1]),
                     reads=[ppgk, "sg"], writes=[gak])
                b.op("s", lambda E: E.activation(out=gb[:, :], in_=ppu[:, :],
                                                 func=AF.Identity, scale=su[:, it:it + 1]),
                     reads=[ppuk, "su"], writes=[gbk])
                b.op("v", lambda E: E.tensor_mul(out=ga[:, :], in0=ga[:, :],
                                                 in1=gb[:, :]),
                     reads=[gak, gbk], writes=[gak])
                b.op("v", lambda E: E.tensor_reduce(out=lcol[:, 0:1], in_=ga[:, :],
                                                    axis=AX, op=ALU.max,
                                                    apply_absolute_value=True),
                     reads=[gak, "lcol"], writes=["lcol"])
                b.op("v", lambda E: E.tensor_max(out=gcol2[:, :], in0=gcol2[:, :],
                                                 in1=lcol[:, 0:1]),
                     reads=["lcol", "gcol2"], writes=["gcol2"])
                b.dma(lambda E: E.dma_start(
                    out=sc_mid[it * 128:(it + 1) * 128, fsl], in_=ga[:, :]),
                    reads=[gak], writes=["sc_mid.%d.%d" % (fc, it)], q="g")

            pend.append((frozenset([ppgk, ppuk]), tail))

        pst = {"wsl": 0, "pp": 0, "rot": 0}
        cl_banks = [p0, p1, pA, pB]
        cl_keys = ["p0", "p1", "pA", "pB"]
        pend = []
        pend_late = []

        def flush_pend():
            for _, f in pend:
                f()
            del pend[:]

        def flush_for(bk):
            while any(bk in banks for banks, _ in pend):
                banks, f = pend.pop(0)
                f()

        def cl_bank():
            r = pst["rot"]
            pst["rot"] = (r + 1) % 4
            flush_for(cl_keys[r])
            return cl_banks[r], cl_keys[r]

        def flush_one():
            if pend:
                banks, f = pend.pop(0)
                f()

        def flush_late():
            for f in pend_late:
                f()
            del pend_late[:]

        def emit_bc(c2):
            csl2 = slice(c2 * TC, (c2 + 1) * TC)
            for qq in range(4):
                b.dma(lambda E, qq=qq, csl2=csl2: E.dma_start(
                    out=brep_v[:, 4 * qq:4 * qq + 4, :],
                    in_=sc_bcB[4 * qq:4 * qq + 4, csl2].unsqueeze(0)
                    .broadcast_to([128, 4, TC])),
                    reads=["sc_bcB.%d" % c2], writes=["brep"], q="g")
            b.dma(lambda E, csl2=csl2: E.dma_start(
                out=crep_v[:, :, :],
                in_=sc_bcC[:, csl2].unsqueeze(0).broadcast_to([128, NS, TC])),
                reads=["sc_bcC.%d" % c2], writes=["crep"], q="g")

        xsg = {"c": 0, "ct": 0}
        xsq = []

        def xs_load():
            c2 = xsg["c"]
            if c2 >= CH:
                return
            ct = xsg["ct"]
            csl2 = slice(c2 * TC, (c2 + 1) * TC)
            xslot = c2 % 2
            if ct == 0:
                tts = ["sc_xnT.%d" % (4 * c2 + j) for j in range(4)]
                b.dma(lambda E, csl2=csl2, xslot=xslot: E.dma_start(
                    out=xnt_t[xslot][:, :, :],
                    in_=sc_xnT.rearrange("(k p) t -> p k t", p=128)[:, :, csl2]),
                    reads=tts, writes=[xnt_keys[xslot]])
            wv, wk = grabA()
            b.dma(lambda E, ct=ct, wv=wv: E.dma_start(
                out=wv[:, :, :], in_=sc_winb[ct, :, :, :]),
                reads=["sc_winb"], writes=[wk])
            wxp = ct % 4
            b.dma(lambda E, ct=ct, wxp=wxp: E.dma_start(
                out=wx2[:, wxp * 96:(wxp + 1) * 96],
                in_=sc_wx[:, ct * 96:(ct + 1) * 96]),
                reads=["sc_wx"], writes=["wx%d" % wxp])
            xsq.append((wv, wk, ct, csl2, c2, xslot))
            if ct == DT - 1:
                xsg["c"] += 1
                xsg["ct"] = 0
            else:
                xsg["ct"] += 1

        def xs_mm():
            if not xsq:
                return
            wv, wk, ct, csl2, c2, xslot = xsq.pop(0)
            pp, ppk = cl_bank()
            for hk in range(HT):
                b.op("t", lambda E, hk=hk, wv=wv, pp=pp, xslot=xslot: E.matmul(
                    pp[:, :], wv[:, hk, :], xnt_t[xslot][:, hk, :],
                    start=(hk == 0), stop=(hk == HT - 1)),
                    reads=[wk, xnt_keys[xslot]], writes=[ppk])

            def tail(ct=ct, csl2=csl2, c2=c2, pp=pp, ppk=ppk):
                st2 = ct % 2
                b.op("s", lambda E: E.activation(
                    out=xsst[st2], in_=pp[:, :], func=AF.Silu,
                    bias=bin_sb[:, ct:ct + 1]),
                    reads=[ppk, "bin_sb"], writes=["xsst%d" % st2])
                b.dma(lambda E: E.dma_start(
                    out=sc_xs[ct * 128:(ct + 1) * 128, csl2], in_=xsst[st2]),
                    reads=["xsst%d" % st2], writes=["sc_xs.%d.%d" % (c2, ct)], q="g")
                b.op("s", lambda E: E.activation(
                    out=xsb0[:, :], in_=pp[:, :], func=AF.Silu,
                    bias=bin_sb[:, ct:ct + 1]),
                    reads=[ppk, "bin_sb"], writes=["xsb0"])
                wxp = ct % 4
                b.op("t", lambda E, wxp=wxp: E.matmul(
                    p2[:, :], wx2[:, wxp * 96:(wxp + 1) * 96], xsb0[:, :],
                    start=(ct == 0), stop=(ct == DT - 1)),
                    reads=["wx%d" % wxp, "xsb0"], writes=["p2"])

            def late(csl2=csl2, c2=c2):
                b.op("s", lambda E: E.activation(out=xp[:, :], in_=p2[:, :],
                                                 func=AF.Identity, bias=bx_sb[:, 0:1]),
                     reads=["p2", "bx_sb"], writes=["xp"])
                b.op("s", lambda E: E.copy(out=dpb[:, :], in_=xp[0:R, :]),
                     reads=["xp"], writes=["dpb"])
                b.op("s", lambda E: E.copy(out=cbf[:, :], in_=xp[64:96, :]),
                     reads=["xp"], writes=["cbf"])
                b.dma(lambda E: E.dma_start(out=sc_bcB[:, csl2], in_=cbf[0:NS, :]),
                      reads=["cbf"], writes=["sc_bcB.%d" % c2], q="g")
                b.dma(lambda E: E.dma_start(out=sc_bcC[:, csl2],
                                            in_=cbf[NS:2 * NS, :]),
                      reads=["cbf"], writes=["sc_bcC.%d" % c2], q="g")

            pend.append((frozenset([ppk]), tail))
            if ct == DT - 1:
                pend_late.append(late)

        wog = {"c": -1, "ht": 0}
        woq = []

        def wout_load():
            c1 = wog["c"]
            if c1 < 0 or wog["ht"] >= HT:
                return
            ht = wog["ht"]
            wog["ht"] += 1
            wa, ak = grabB()
            wb2, bk2 = grabB()
            b.dma(lambda E, ht=ht, wa=wa: E.dma_start(
                out=wa[:, :, :], in_=sc_woutb[ht, :, 0:8, :]),
                reads=["sc_woutb"], writes=[ak], q="s")
            b.dma(lambda E, ht=ht, wb2=wb2: E.dma_start(
                out=wb2[:, :, :], in_=sc_woutb[ht, :, 8:16, :]),
                reads=["sc_woutb"], writes=[bk2], q="s")
            woq.append((wa, ak, wb2, bk2, ht, c1))

        def wout_mm():
            if not woq:
                return
            wa, ak, wb2, bk2, ht, c1 = woq.pop(0)
            yv = ybf_t[c1 % 2]
            yk = ybf_keys[c1 % 2]
            csl1 = slice(c1 * TC, (c1 + 1) * TC)
            pp, ppk = cl_bank()
            for dk in range(DT):
                w_ap = wa[:, dk, :] if dk < 8 else wb2[:, dk - 8, :]
                wk = ak if dk < 8 else bk2
                b.op("t", lambda E, dk=dk, w_ap=w_ap, wk=wk, yv=yv, pp=pp: E.matmul(
                    pp[:, :], w_ap, yv[:, dk, :],
                    start=(dk == 0), stop=(dk == DT - 1)),
                    reads=[wk, yk], writes=[ppk])

            def tail(ht=ht, csl1=csl1, c1=c1, pp=pp, ppk=ppk):
                par = ht % 2
                b.op("s", lambda E: E.activation(
                    out=stgpair[par][:, :], in_=pp[:, :], func=AF.Identity,
                    bias=bout_sb[:, ht:ht + 1]),
                    reads=[ppk, "bout_sb"], writes=[stgkey[par]])
                b.dma(lambda E: E.dma_start(
                    out=sc_part[ht * 128:(ht + 1) * 128, csl1], in_=stgpair[par][:, :]),
                    reads=[stgkey[par]], writes=["sc_part.%d.%d" % (c1, ht)], q="g")

            pend.append((frozenset([ppk]), tail))

        gzq = []

        def gz_load(dts):
            for d2 in dts:
                ct = DT + d2
                wv, wk = grabA()
                b.dma(lambda E, ct=ct, wv=wv: E.dma_start(
                    out=wv[:, :, :], in_=sc_winb[ct, :, :, :]),
                    reads=["sc_winb"], writes=[wk])
                gzq.append((wv, wk, d2))

        def gz_mm(n2, xv, xk):
            got = []
            for _ in range(n2):
                wv, wk, d2 = gzq.pop(0)
                pp, ppk = cl_bank()
                got.append((d2, pp, ppk))
                for hk in range(HT):
                    b.op("t", lambda E, hk=hk, pp=pp, wv=wv, xv=xv: E.matmul(
                        pp[:, :], wv[:, hk, :], xv[:, hk, :],
                        start=(hk == 0), stop=(hk == HT - 1)),
                        reads=[wk, xk], writes=[ppk])

            def tail(got=got):
                for d2, pp, ppk in got:
                    p2r = d2 % 2
                    ct = DT + d2
                    b.op("s", lambda E, ct=ct, p2r=p2r, pp=pp, ppk=ppk: E.activation(
                        out=gzpair[p2r][:, :], in_=pp[:, :], func=AF.Silu,
                        bias=bin_sb[:, ct:ct + 1]),
                        reads=[ppk, "bin_sb"], writes=[gzkey[p2r]])

            pend.append((frozenset(k for _, _, k in got), tail))

        p3pair = [p3, ptfA]
        p3keys = ["p3", "p3b"]

        def delta_chain(dt):
            par = dt % 2
            pd, pdk = p3pair[par], p3keys[par]
            b.op("t", lambda E, dt=dt, pd=pd: E.matmul(
                pd[:, :], wdt_sb[:, dt * 128:(dt + 1) * 128], dpb[:, :],
                start=True, stop=True),
                reads=["wdt_sb", "dpb"], writes=[pdk])
            b.op("s", lambda E, dt=dt, pd=pd: E.activation(
                out=dA0[:, :], in_=pd[:, :], func=AF.Exp,
                bias=bdt_sb[:, dt:dt + 1]),
                reads=[pdk, "bdt_sb"], writes=["dA0"])
            b.op("s", lambda E, par=par: E.activation(
                out=dpair[par][:, :], in_=dA0[:, :], func=AF.Ln, bias=1.0),
                reads=["dA0"], writes=[dkey[par]])

        b.fork(["ptb0"], ["pA"])
        b.fork(["ptb1"], ["pB"])
        for _ in range(DT):
            xs_load()
            xs_mm()
            flush_pend()
        flush_late()
        emit_bc(0)

        for c in range(CH):
            csl = slice(c * TC, (c + 1) * TC)
            xslot = c % 2
            xk = xnt_keys[xslot]
            xv = xnt_t[xslot]
            ybf_cur = ybf_t[c % 2]
            ybk = ybf_keys[c % 2]
            wog["c"] = c - 1
            wog["ht"] = 0
            flush_pend()
            if c == 0:
                gz_load((0, 1))
            gz_mm(2, xv, xk)
            if c == 0:
                delta_chain(0)
            flush_pend()

            for dt in range(DT):
                par = dt % 2
                if dt % 2 == 1 and dt + 2 < DT:
                    gz_load((dt + 1, dt + 2))
                if dt == DT - 1 and c + 1 < CH:
                    gz_load((0, 1))
                f1_load()
                if dt % 2 == 0:
                    f1_load()
                if xsg["c"] == c + 1:
                    xs_load()
                if dt < 7 and xsg["c"] == c + 1:
                    xs_load()
                if dt % 2 == 1:
                    wout_load()
                if dt == 0 and c == 0:
                    for d0 in (0, 1):
                        b.dma(lambda E, d0=d0, csl=csl, c=c: E.dma_start(
                            out=xsld[d0], in_=sc_xs[d0 * 128:(d0 + 1) * 128, csl]),
                            reads=["sc_xs.%d.%d" % (c, d0)], writes=["xsld%d" % d0])
                b.op("v", lambda E, dt=dt, par=par: E.tensor_mul(
                    out=dxb[:, :], in0=dpair[par][:, :], in1=xsld[dt % 2]),
                    reads=[dkey[par], "xsld%d" % (dt % 2)], writes=["dxb"])
                def emit_exps(g):
                    gs = g % 2
                    da, dak = dA4p[gs], dA4keys[gs]
                    for j in range(4):
                        n = 4 * g + j
                        b.op("s", lambda E, n=n, j=j, par=par, da=da: E.activation(
                            out=da[:, j * TC:(j + 1) * TC], in_=dpair[par][:, :],
                            func=AF.Exp,
                            scale=A_sb[:, dt * NS + n:dt * NS + n + 1]),
                            reads=[dkey[par], "A_sb"], writes=[dak])

                emit_exps(0)
                for g in range(4):
                    gs = g % 2
                    da, dak, dav = dA4p[gs], dA4keys[gs], dA4v[gs]
                    base = dt * NS + 4 * g
                    if g + 1 < 4:
                        emit_exps(g + 1)
                    b.op("v", lambda E, g=g: E.tensor_mul(
                        out=u4v[:, :, :],
                        in0=dxb[:, :].unsqueeze(1).broadcast_to([128, 4, TC]),
                        in1=brep_v[:, 4 * g:4 * g + 4, :]),
                        reads=["dxb", "brep"], writes=["u4"])
                    b.op("v", lambda E, dav=dav, base=base: E.tensor_mul(
                        out=lcol[:, 4:8].unsqueeze(2), in0=dav[:, :, 0:1],
                        in1=carry[:, base:base + 4].unsqueeze(2)),
                        reads=[dak, "carry", "lcol"], writes=["lcol"])
                    b.op("v", lambda E, dav=dav: E.tensor_scalar_mul(
                        out=dav[:, :, 0:1], in0=dav[:, :, 0:1], scalar1=0.0),
                        reads=[dak], writes=[dak])
                    b.op("v", lambda E: E.tensor_add(
                        out=u4v[:, :, 0:1], in0=u4v[:, :, 0:1],
                        in1=lcol[:, 4:8].unsqueeze(2)),
                        reads=["u4", "lcol"], writes=["u4"])
                    b.op("v", lambda E, g=g, da=da: E.tensor_tensor_scan(
                        out=h_a[:, 4 * g * TC:(4 * g + 4) * TC], data0=da[:, :],
                        data1=u4flat[:, :], initial=zc[:, 0:1],
                        op0=ALU.mult, op1=ALU.add),
                        reads=[dak, "u4", "zc"], writes=["h"])
                    flush_one()
                    if g == 1:
                        flush_one()
                if dt + 1 < DT:
                    delta_chain(dt + 1)
                flush_pend()
                if dt == DT - 2:
                    flush_late()
                    if c + 1 < CH:
                        delta_chain(0)
                b.op("s", lambda E, dt=dt: E.copy(
                    out=carry[:, dt * NS:(dt + 1) * NS],
                    in_=h_v[:, :, TC - 1]),
                    reads=["h"], writes=["carry"])
                b.op("v", lambda E: E.tensor_mul(out=h_a[:, :], in0=h_a[:, :],
                                                 in1=crep_a[:, :]),
                     reads=["h", "crep"], writes=["h"])
                b.op("v", lambda E: E.tensor_add(out=h_a[:, 0:4096], in0=h_a[:, 0:4096],
                                                 in1=h_a[:, 4096:8192]),
                     reads=["h"], writes=["h"])
                b.op("v", lambda E: E.tensor_add(out=h_a[:, 0:2048], in0=h_a[:, 0:2048],
                                                 in1=h_a[:, 2048:4096]),
                     reads=["h"], writes=["h"])
                b.op("v", lambda E: E.tensor_add(out=h_a[:, 0:1024], in0=h_a[:, 0:1024],
                                                 in1=h_a[:, 1024:2048]),
                     reads=["h"], writes=["h"])
                b.op("v", lambda E: E.tensor_add(out=dxb[:, :], in0=h_a[:, 0:TC],
                                                 in1=h_a[:, TC:2 * TC]),
                     reads=["h"], writes=["dxb"])
                b.op("v", lambda E, dt=dt: E.scalar_tensor_tensor(
                    out=u0[:, :], in0=xsld[dt % 2], scalar=dpv_sb[:, dt:dt + 1],
                    in1=dxb[:, :], op0=ALU.mult, op1=ALU.add),
                    reads=["xsld%d" % (dt % 2), "dpv_sb", "dxb"], writes=["u0"])
                b.op("v", lambda E, dt=dt, par=par, ybf_cur=ybf_cur: E.tensor_mul(
                    out=ybf_cur[:, dt, :], in0=u0[:, :], in1=gzpair[par][:, :]),
                    reads=["u0", gzkey[par]], writes=[ybk])
                if dt == DT - 1 and c + 1 < CH:
                    emit_bc(c + 1)
                    for d0 in (0, 1):
                        b.dma(lambda E, d0=d0, c=c: E.dma_start(
                            out=xsld[d0],
                            in_=sc_xs[d0 * 128:(d0 + 1) * 128,
                                      (c + 1) * TC:(c + 2) * TC]),
                            reads=["sc_xs.%d.%d" % (c + 1, d0)],
                            writes=["xsld%d" % d0])
                if dt + 2 < DT:
                    b.dma(lambda E, dt=dt, csl=csl, c=c: E.dma_start(
                        out=xsld[dt % 2],
                        in_=sc_xs[(dt + 2) * 128:(dt + 3) * 128, csl]),
                        reads=["sc_xs.%d.%d" % (c, dt + 2)], writes=["xsld%d" % (dt % 2)])
                if dt % 2 == 1 and dt + 2 < DT:
                    gz_mm(2, xv, xk)
                f1_mm()
                if dt % 2 == 0:
                    f1_mm()
                xs_mm()
                xs_mm()
                if dt % 2 == 1:
                    wout_mm()

        wog["c"] = CH - 1
        wog["ht"] = 0
        for _ in range(HT):
            wout_load()
            wout_mm()

        def f2_loads(c2):
            csl2 = slice(c2 * TC, (c2 + 1) * TC)
            midkeys = ["sc_mid.%d.%d" % (c2, it) for it in range(IT)]
            b.dma(lambda E, csl2=csl2: E.dma_start(
                out=xs_v[:, :, :],
                in_=sc_mid.rearrange("(k p) t -> p k t", p=128)[:, 0:16, csl2]),
                reads=midkeys[0:16], writes=["xs"], q="s")
            b.dma(lambda E, csl2=csl2: E.dma_start(
                out=brep_f[:, 0:6 * TC].rearrange("p (k t) -> p k t", k=6),
                in_=sc_mid.rearrange("(k p) t -> p k t", p=128)[:, 16:22, csl2]),
                reads=midkeys[16:22], writes=["brep"], q="s")

        b.fork(["xsld0", "xsld1", "xnt1", "ybf1", "xsst0", "xsst1"], ["xs"])
        f2_loads(0)

        while f1L["i"] < F1_STEPS or f1q:
            f1_load()
            f1_mm()
            flush_pend()
        flush_pend()

        allreduce_scalar(gcol2, "gcol2", cc2_in, cc2_out, 8, 2, "cc2", 2)
        b.op("v", lambda E: E.tensor_scalar(out=sd[:, :], in0=srd[:, :], scalar1=cols[:, 2:3],
                                            scalar2=None, op0=ALU.mult),
             reads=["srd", "cols"], writes=["sd"])

        b.fork(["p3b"], ["ptf0"])
        b.fork(["dA4_0", "dA4_1"], ["sum.%d" % ht for ht in range(HT)])
        b.fork(["u4", "zxq"], ["z"])
        b.fork(["wA0", "wA1"], ["wstr0"])
        b.fork(["wA2", "wA3"], ["wstr1"])
        b.fork(["ybf0"], ["xtile", "ost"])
        b.fork(["wB0"], ["wd1a"])
        b.fork(["wB1"], ["wd1b"])
        b.fork(["wB2"], ["wd1c"])
        z_q = z_a.rearrange("p (k t) -> p k t", k=NS)
        c_q = crep_a.rearrange("p (k t) -> p k t", k=NS)
        wd_a = wstr0[:, :].rearrange("p (k f) -> p k f", k=DT)
        wd_b = wstr1[:, 0:768].rearrange("p (k f) -> p k f", k=6)
        ybf_f32 = ybf_a.bitcast(F32)
        xtile2 = ybf_f32[:, 0:1024]
        ost = ybf_f32[:, 1024:2048]

        def f2_scale16(c2):
            b.op("s", lambda E: E.activation(out=h_a[:, :], in_=xs_a[:, :],
                                             func=AF.Identity, scale=cols[:, 3:4]),
                 reads=["xs", "cols"], writes=["h"])

        for c in range(CH):
            csl = slice(c * TC, (c + 1) * TC)
            if c == 0:
                f2_scale16(0)
            b.op("v", lambda E: E.tensor_copy(out=i32_xs[:, :], in_=h_a[:, :]),
                 reads=["h"], writes=["xs"])
            b.op("v", lambda E: E.tensor_scalar(out=z_a[:, :], in0=i32_xs[:, :],
                                                scalar1=-(QMAXI + 1.0), scalar2=QMAXI,
                                                op0=ALU.max, op1=ALU.min),
                 reads=["xs"], writes=["z"])
            b.op("s", lambda E: E.activation(out=h_a[:, 0:6 * TC], in_=brep_f[:, 0:6 * TC],
                                             func=AF.Identity, scale=cols[:, 3:4]),
                 reads=["brep", "cols"], writes=["h"])
            b.op("v", lambda E: E.tensor_copy(out=i32_brep[:, 0:6 * TC], in_=h_a[:, 0:6 * TC]),
                 reads=["h"], writes=["brep"])
            b.op("v", lambda E: E.tensor_scalar(out=crep_a[:, 0:6 * TC],
                                                in0=i32_brep[:, 0:6 * TC],
                                                scalar1=-(QMAXI + 1.0), scalar2=QMAXI,
                                                op0=ALU.max, op1=ALU.min),
                 reads=["brep"], writes=["crep"])

            if c + 1 < CH:
                f2_loads(c + 1)
                f2_scale16(c + 1)

            def wd_load(ht):
                if ht % 2 == 0:
                    b.dma(lambda E, ht=ht: E.dma_start(
                        out=wd_a[:, :, :], in_=sc_wdT[ht, :, 0:16, :]),
                        reads=["sc_wdT"], writes=["wstr0"])
                    b.dma(lambda E, ht=ht: E.dma_start(
                        out=wd_b[:, :, :], in_=sc_wdT[ht, :, 16:22, :]),
                        reads=["sc_wdT"], writes=["wstr1"])
                else:
                    b.dma(lambda E, ht=ht: E.dma_start(
                        out=wpB[0][:, :, :], in_=sc_wdT[ht, :, 0:8, :]),
                        reads=["sc_wdT"], writes=["wd1a"])
                    b.dma(lambda E, ht=ht: E.dma_start(
                        out=wpB[1][:, :, :], in_=sc_wdT[ht, :, 8:16, :]),
                        reads=["sc_wdT"], writes=["wd1b"])
                    b.dma(lambda E, ht=ht: E.dma_start(
                        out=wpB[2][:, 0:6, :], in_=sc_wdT[ht, :, 16:22, :]),
                        reads=["sc_wdT"], writes=["wd1c"])

            def wd_tile(ht, ik):
                if ht % 2 == 0:
                    if ik < 16:
                        return wd_a[:, ik, :], "wstr0"
                    return wd_b[:, ik - 16, :], "wstr1"
                if ik < 8:
                    return wpB[0][:, ik, :], "wd1a"
                if ik < 16:
                    return wpB[1][:, ik - 8, :], "wd1b"
                return wpB[2][:, ik - 16, :], "wd1c"

            wd_load(0)
            for ht in range(HT):
                par = ht % 2
                if ht + 1 < HT:
                    wd_load(ht + 1)
                b.dma(lambda E, ht=ht, csl=csl, par=par: E.dma_start(
                    out=stgpair[par][:, :], in_=sc_part[ht * 128:(ht + 1) * 128, csl]),
                    reads=["sc_part.%d.%d" % (c, ht)], writes=[stgkey[par]], q="y")
                pk = "p%d" % par
                for ik in range(IT):
                    w_ap, rk = wd_tile(ht, ik)
                    rhs_ap, qk = (z_q[:, ik, :], "z") if ik < 16 else                         (c_q[:, ik - 16, :], "crep")
                    b.op("t", lambda E, w_ap=w_ap, rhs_ap=rhs_ap, ik=ik, par=par: E.matmul(
                        ppair[par][:, :], w_ap, rhs_ap,
                        start=(ik == 0), stop=(ik == IT - 1)),
                        reads=[rk, qk], writes=[pk])
                sb_out = dA4a if ht < 4 else dA4b
                b.op("v", lambda E, ht=ht, par=par, sb_out=sb_out: E.scalar_tensor_tensor(
                    out=sb_out[:, (ht % 4) * TC:(ht % 4 + 1) * TC],
                    in0=ppair[par][:, :], scalar=sd[:, ht:ht + 1], in1=stgpair[par][:, :],
                    op0=ALU.mult, op1=ALU.add),
                    reads=[pk, "sd", stgkey[par]], writes=["sum.%d" % ht])

            for tbl in range(4):
                tb = 4 * c + tbl
                tsl = slice(tb * 128, (tb + 1) * 128)
                b.dma(lambda E, tsl=tsl: E.dma_start(out=xtile2, in_=din["x"][tsl, :]),
                      writes=["xtile"])
                for ht in range(HT):
                    pf = [ptfA[:, 0:128], ptfB][ht % 2]
                    pfk = "ptf%d" % (ht % 2)
                    sb_in = dA4a if ht < 4 else dA4b
                    off = (ht % 4) * TC + tbl * 128
                    b.op("t", lambda E, off=off, pf=pf, sb_in=sb_in: E.transpose(
                        pf[:, :], sb_in[:, off:off + 128], idf[:, :]),
                        reads=["sum.%d" % ht, "idf"], writes=[pfk])
                    b.op("v", lambda E, ht=ht, pf=pf: E.tensor_add(
                        out=ost[:, ht * 128:(ht + 1) * 128],
                        in0=pf[:, :],
                        in1=xtile2[:, ht * 128:(ht + 1) * 128]),
                        reads=[pfk, "xtile"], writes=["ost"])
                b.dma(lambda E, tsl=tsl: E.dma_start(out=dout[tsl, :], in_=ost[:, :]),
                      reads=["ost"], writes=["out.%d" % tb], q="g")

        for dsem in sorted(dysems) + sorted(dgsems) + sorted(dssems):
            if b.cnt[dsem]:
                b.raw_wait("y", dsem, b.cnt[dsem])

        @block.vector
        def _(E):
            for f in b.plan["v"]:
                f(E)

        @block.scalar
        def _(E):
            for f in b.plan["s"]:
                f(E)

        @block.tensor
        def _(E):
            for f in b.plan["t"]:
                f(E)

        @block.gpsimd
        def _(E):
            for f in b.plan["g"]:
                f(E)

        @block.sync
        def _(E):
            for f in b.plan["y"]:
                f(E)

    return nc


def kernel(**inputs):
    x = np.ascontiguousarray(np.asarray(inputs["L0_attn_x"], dtype=np.float32))
    names = {
        "Win": "L0_attn_Win", "bin": "L0_attn_bin", "Wx": "L0_attn_Wx",
        "bx": "L0_attn_bx", "Wdt": "L0_attn_Wdt", "bdt": "L0_attn_bdt",
        "A_log": "L0_attn_A_log", "Dp": "L0_attn_Dp", "Wout": "L0_attn_Wout",
        "bout": "L0_attn_bout", "clip_x": "L0_ffn_clip_x",
        "clip_down": "L0_ffn_clip_down", "w_gate": "L0_ffn_w_gate",
        "w_up": "L0_ffn_w_up", "w_down": "L0_ffn_w_down",
    }
    base = {k: np.ascontiguousarray(np.asarray(inputs[v], dtype=np.float32))
            for k, v in names.items()}
    base["bin"] = np.ascontiguousarray(base["bin"].reshape(32, 128).T)
    base["bdt"] = np.ascontiguousarray(base["bdt"].reshape(DT, 128).T)
    base["Dp"] = np.ascontiguousarray(base["Dp"].reshape(DT, 128).T)
    base["bout"] = np.ascontiguousarray(base["bout"].reshape(HT, 128).T)
    base["A_log"] = np.ascontiguousarray(
        base["A_log"].reshape(DT, 128, NS).transpose(1, 0, 2).reshape(128, DT * NS))
    base["Wx"] = np.ascontiguousarray(
        base["Wx"].reshape(DT, 128, 96).transpose(1, 0, 2).reshape(128, DT * 96))
    base["ident_b"] = np.eye(128).astype(ml_dtypes.bfloat16)
    base["ident_f"] = np.eye(128, dtype=np.float32)
    nc = build_program()
    in_maps = [dict(base, x=x[c]) for c in range(8)]
    res = run_bass_kernel_spmd(nc, in_maps, core_ids=list(range(8)))
    return np.stack([np.asarray(res.results[c]["out"]) for c in range(8)], axis=0)


# version 53
# speedup vs baseline: 1.0046x; 1.0046x over previous
import numpy as np
import ml_dtypes
import concourse.bass as bass
import concourse.mybir as mybir
from concourse.bass_utils import run_bass_kernel_spmd

F32 = mybir.dt.float32
BF16 = mybir.dt.bfloat16
I32 = mybir.dt.int32
AF = mybir.ActivationFunctionType
ALU = mybir.AluOpType
AX = mybir.AxisListType.X

T = 4096
H = 1024
D = 2048
NS = 16
R = 64
II = 2816
TC = 512
CH = T // TC
DT = D // 128
HT = H // 128
IT = II // 128
NT = T // 128
QMAXI = 127.0


class Bld:

    ENG = ("v", "s", "t", "g", "y")

    def __init__(self, nc, sems, dpool):
        self.nc = nc
        self.sems = sems
        self.dpool = dpool
        self.dct = {q: 0 for q in dpool}
        self.cnt = {k: 0 for k in sems}
        self.wm = {e: {k: 0 for k in sems} for e in self.ENG}
        self.bufs = {}
        self.plan = {e: [] for e in self.ENG}

    def _need(self, reads, writes):
        need = {}

        def merge(d):
            for s, v in d.items():
                if v > need.get(s, 0):
                    need[s] = v

        for bk in reads:
            e = self.bufs.get(bk)
            if e:
                merge(e["w"])
        for bk in writes:
            e = self.bufs.get(bk)
            if e:
                merge(e["w"])
                merge(e["r"])
        return need

    def _waits(self, eng, need):
        wm = self.wm[eng]
        out = []
        for s, v in sorted(need.items()):
            if wm.get(s, 0) < v:
                out.append((s, v))
                wm[s] = v
        return out

    def _record(self, reads, writes, sem, val):
        for bk in writes:
            self.bufs[bk] = {"w": {sem: val}, "r": {}}
        for bk in reads:
            e = self.bufs.setdefault(bk, {"w": {}, "r": {}})
            if val > e["r"].get(sem, 0):
                e["r"][sem] = val

    def op(self, eng, fn, reads=(), writes=()):
        waits = self._waits(eng, self._need(reads, writes))
        self.cnt[eng] += 1
        val = self.cnt[eng]
        sems = self.sems

        def thunk(E, waits=tuple(waits), fn=fn, sem=eng):
            for s, v in waits:
                E.wait_ge(sems[s], v)
            fn(E).then_inc(sems[sem], 1)

        self.plan[eng].append(thunk)
        self._record(reads, writes, eng, val)

    def dma(self, fn, reads=(), writes=(), q="y"):
        pool = self.dpool[q]
        dsem = pool[self.dct[q] % len(pool)]
        self.dct[q] += 1
        waits = self._waits(q, self._need(reads, writes))
        self.cnt[dsem] += 16
        val = self.cnt[dsem]
        sems = self.sems

        def thunk(E, waits=tuple(waits), fn=fn, dsem=dsem):
            for s, v in waits:
                E.wait_ge(sems[s], v)
            fn(E).then_inc(sems[dsem], 16)

        self.plan[q].append(thunk)
        self._record(reads, writes, dsem, val)

    def fork(self, srcs, dsts):
        rec = {}
        for s in srcs:
            e = self.bufs.get(s)
            if e:
                for sem, v in list(e["w"].items()) + list(e["r"].items()):
                    if v > rec.get(sem, 0):
                        rec[sem] = v
        for d in dsts:
            old = self.bufs.get(d)
            neww = dict(rec)
            if old:
                for sem, v in list(old["w"].items()) + list(old["r"].items()):
                    if v > neww.get(sem, 0):
                        neww[sem] = v
            self.bufs[d] = {"w": neww, "r": {}}

    def raw_wait(self, eng, sem, val):
        sems = self.sems

        def thunk(E, sem=sem, val=val):
            E.wait_ge(sems[sem], val)

        self.plan[eng].append(thunk)
        if self.wm[eng].get(sem, 0) < val:
            self.wm[eng][sem] = val


def build_program():
    nc = bass.Bass("TRN2", target_bir_lowering=False)

    din = {}
    for name, shape in [
        ("x", [T, H]), ("Win", [H, 4 * H]), ("bin", [128, 32]),
        ("Wx", [128, DT * 96]), ("bx", [96]), ("Wdt", [R, D]), ("bdt", [128, DT]),
        ("A_log", [128, DT * NS]), ("Dp", [128, DT]), ("Wout", [D, H]),
        ("bout", [128, HT]),
        ("clip_x", [1]), ("clip_down", [1]),
        ("w_gate", [II, H]), ("w_up", [II, H]), ("w_down", [H, II]),
        ("ident_b", [128, 128]), ("ident_f", [128, 128]),
    ]:
        dt = BF16 if name == "ident_b" else F32
        din[name] = nc.dram_tensor(name, shape, dt, kind="ExternalInput").ap()
    dout = nc.dram_tensor("out", [T, H], F32, kind="ExternalOutput").ap()

    sc_xn = nc.dram_tensor("sc_xn", [T, H], F32).ap()
    sc_xnT = nc.dram_tensor("sc_xnT", [H, T], BF16).ap()
    sc_xqT = nc.dram_tensor("sc_xqT", [H, T], BF16).ap()
    sc_winb = nc.dram_tensor("sc_winb", [32, 128, HT, 128], BF16).ap()
    sc_woutb = nc.dram_tensor("sc_woutb", [HT, 128, DT, 128], BF16).ap()
    sc_wgT = nc.dram_tensor("sc_wgT", [IT, 128, HT, 128], BF16).ap()
    sc_wuT = nc.dram_tensor("sc_wuT", [IT, 128, HT, 128], BF16).ap()
    sc_wdT = nc.dram_tensor("sc_wdT", [HT, 128, IT, 128], BF16).ap()
    sc_xs = nc.dram_tensor("sc_xs", [D, T], F32).ap()
    sc_bcB = nc.dram_tensor("sc_bcB", [NS, T], BF16).ap()
    sc_bcC = nc.dram_tensor("sc_bcC", [NS, T], BF16).ap()
    sc_part = nc.dram_tensor("sc_part", [H, T], F32).ap()
    sc_mid = nc.dram_tensor("sc_mid", [II, T], F32).ap()
    sc_col = nc.dram_tensor("sc_col", [128], F32).ap()
    sc_wx = nc.dram_tensor("sc_wx", [128, DT * 96], BF16).ap()
    sc_s = nc.dram_tensor("sc_s", [8], F32).ap()
    cc1_in = nc.dram_tensor("cc1_in", [1, 1], F32).ap()
    cc1_out = nc.dram_tensor("cc1_out", [1, 1], F32).ap()
    cc2_in = nc.dram_tensor("cc2_in", [1, 1], F32).ap()
    cc2_out = nc.dram_tensor("cc2_out", [1, 1], F32).ap()

    import contextlib
    with contextlib.ExitStack() as _stack:
        def _sb(name, shape, dtype):
            return _stack.enter_context(nc.sbuf_tensor(name, shape, dtype))

        def _ps(name, shape, dtype):
            return _stack.enter_context(nc.psum_tensor(name, shape, dtype))

        def _sem(name):
            return _stack.enter_context(nc.semaphore(name))

        xs_a = _sb("xs_a", [128, 8192], F32)
        brep_a = _sb("brep_a", [128, 8192], BF16)
        h_a = _sb("h_a", [128, 8192], F32)
        z_a = _sb("z_a", [128, 8192], BF16)
        crep_a = _sb("crep_a", [128, 8192], BF16)
        ybf_a = _sb("ybf_a", [128, 8192], BF16)
        xnt = _sb("xnt", [128, 4096], BF16)
        wstr0 = _sb("wstr0", [128, 2048], BF16)
        wstr1 = _sb("wstr1", [128, 2048], BF16)
        delta0 = _sb("delta0", [128, TC], F32)
        delta1 = _sb("delta1", [128, TC], F32)
        gz0 = _sb("gz0", [128, TC], F32)
        gz1 = _sb("gz1", [128, TC], F32)
        dA0 = _sb("dA0", [128, TC], F32)
        dA4a = _sb("dA4a", [128, 4 * TC], F32)
        dA4b = _sb("dA4b", [128, 4 * TC], F32)
        zc = _sb("zc", [128, 1], F32)
        u0 = _sb("u0", [128, TC], F32)
        dxb = _sb("dxb", [128, TC], F32)
        stg0 = _sb("stg0", [128, TC], F32)
        stg1 = _sb("stg1", [128, TC], F32)
        xsb0 = _sb("xsb0", [128, TC], BF16)
        xsb1 = _sb("xsb1", [128, TC], BF16)
        dpb = _sb("dpb", [R, TC], BF16)
        xp = _sb("xp", [96, TC], F32)
        A_sb = _sb("A_sb", [128, DT * NS], F32)
        carry = _sb("carry", [128, DT * NS], F32)
        bin_sb = _sb("bin_sb", [128, 32], F32)
        bdt_sb = _sb("bdt_sb", [128, DT], F32)
        dpv_sb = _sb("dpv_sb", [128, DT], F32)
        bout_sb = _sb("bout_sb", [128, HT], F32)
        bx_sb = _sb("bx_sb", [96, 1], F32)
        srg = _sb("srg", [128, IT], F32)
        sru = _sb("sru", [128, IT], F32)
        srd = _sb("srd", [128, HT], F32)
        sg = _sb("sg", [128, IT], F32)
        su = _sb("su", [128, IT], F32)
        sd = _sb("sd", [128, HT], F32)
        cols = _sb("cols", [128, 4], F32)
        lcol = _sb("lcol", [128, 8], F32)
        lcol2 = _sb("lcol2", [128, 8], F32)
        gcol = _sb("gcol", [128, 1], F32)
        gcol2 = _sb("gcol2", [128, 1], F32)
        tiny = _sb("tiny", [1, 16], F32)
        idb = _sb("idb", [128, 128], BF16)
        idf = _sb("idf", [128, 128], F32)
        wx2 = _sb("wx2", [128, 384], BF16)
        wdt_sb = _sb("wdt_sb", [R, D], BF16)
        cst = _sb("cst", [128, 2], F32)
        fga = _sb("fga", [128, TC], F32)
        fgb = _sb("fgb", [128, TC], F32)
        fga2 = _sb("fga2", [128, TC], F32)
        fgb2 = _sb("fgb2", [128, TC], F32)
        wfgt = _sb("wfgt", [128, 1024], BF16)
        wfut = _sb("wfut", [128, 1024], BF16)
        wfgt2 = _sb("wfgt2", [128, 1024], BF16)
        wfut2 = _sb("wfut2", [128, 1024], BF16)
        p0 = _ps("p0", [128, TC], F32)
        p1 = _ps("p1", [128, TC], F32)
        p2 = _ps("p2", [96, TC], F32)
        p3 = _ps("p3", [128, TC], F32)
        pA = _ps("pA", [128, TC], F32)
        pB = _ps("pB", [128, TC], F32)
        ptfA = _ps("ptfA", [128, 512], F32)
        ptfB = _ps("ptfB", [128, 128], F32)
        sem_v = _sem("sem_v")
        sem_s = _sem("sem_s")
        sem_t = _sem("sem_t")
        sem_g = _sem("sem_g")
        sem_y = _sem("sem_y")
        NDQ = 12
        dysems = {"dy%d" % i: _sem("sem_dy%d" % i) for i in range(NDQ)}
        dgsems = {"dg%d" % i: _sem("sem_dg%d" % i) for i in range(6)}
        dssems = {"ds%d" % i: _sem("sem_ds%d" % i) for i in range(4)}
        block = _stack.enter_context(nc.Block())

        sems = {"v": sem_v, "s": sem_s, "t": sem_t, "g": sem_g, "y": sem_y}
        sems.update(dysems)
        sems.update(dgsems)
        sems.update(dssems)
        b = Bld(nc, sems, {"y": sorted(dysems), "g": sorted(dgsems),
                           "s": sorted(dssems)})

        ppair = [p0, p1]
        wpair = [wstr0, wstr1]
        wkey = ["wstr0", "wstr1"]
        dA4p = [dA4a, dA4b]
        dA4keys = ["dA4_0", "dA4_1"]
        dA4v = [dA4a.rearrange("p (k t) -> p k t", k=4),
                dA4b.rearrange("p (k t) -> p k t", k=4)]
        dpair = [delta0, delta1]
        dkey = ["delta0", "delta1"]
        gzpair = [gz0, gz1]
        gzkey = ["gz0", "gz1"]
        stgpair = [stg0, stg1]
        stgkey = ["stg0", "stg1"]

        h_v = h_a.rearrange("p (n t) -> p n t", n=NS)
        h_red = h_a.rearrange("p (n t) -> p t n", n=NS)
        ptb_pair = [pA.bitcast(BF16)[:, 0:128], pB.bitcast(BF16)[:, 0:128]]
        xtile = h_a[:, 2048:3072]
        xsa_bf = xs_a.bitcast(BF16)
        xnt_t = [xnt.rearrange("p (k t) -> p k t", k=HT),
                 xsa_bf[:, 2048:6144].rearrange("p (k t) -> p k t", k=HT)]
        xnt_keys = ["xnt0", "xnt1"]
        ybf_t = [ybf_a.rearrange("p (d t) -> p d t", d=DT),
                 xsa_bf[:, 6144:14336].rearrange("p (d t) -> p d t", d=DT)]
        ybf_keys = ["ybf0", "ybf1"]
        xsld = [xs_a[:, 0:TC], xs_a[:, TC:2 * TC]]
        cbf = xsa_bf[0:2 * NS, 14336:14848]
        xsst = [xs_a[:, 7168:7680], xs_a[:, 7680:8192]]
        wl = [wstr0[:, 0:1024].rearrange("p (k f) -> p k f", k=HT),
              wstr1[:, 0:1024].rearrange("p (k f) -> p k f", k=HT)]
        zf32 = z_a.bitcast(F32)
        u4flat = zf32[:, 0:4 * TC]
        u4v = u4flat.rearrange("p (k t) -> p k t", k=4)
        ukeys = ["zu0", "zu1"]
        xqv = z_a[:, 4096:8192].rearrange("p (k t) -> p k t", k=HT)
        wpA = [wstr0[:, 0:1024].rearrange("p (k f) -> p k f", k=HT),
               wstr0[:, 1024:2048].rearrange("p (k f) -> p k f", k=HT),
               wstr1[:, 0:1024].rearrange("p (k f) -> p k f", k=HT),
               wstr1[:, 1024:2048].rearrange("p (k f) -> p k f", k=HT)]
        wpA_keys = ["wA0", "wA1", "wA2", "wA3"]
        wpB = [wfgt[:, :].rearrange("p (k f) -> p k f", k=HT),
               wfut[:, :].rearrange("p (k f) -> p k f", k=HT),
               wfgt2[:, :].rearrange("p (k f) -> p k f", k=HT),
               wfut2[:, :].rearrange("p (k f) -> p k f", k=HT)]
        wpB_keys = ["wB0", "wB1", "wB2", "wB3"]
        rotAB = {"A": 0, "B": 0}

        def grabA():
            r = rotAB["A"] % 4
            rotAB["A"] += 1
            return wpA[r], wpA_keys[r]

        def grabB():
            r = rotAB["B"] % 4
            rotAB["B"] += 1
            return wpB[r], wpB_keys[r]
        xs_v = xs_a.rearrange("p (d t) -> p d t", d=DT)
        brep_v = brep_a.rearrange("p (n t) -> p n t", n=NS)
        brep_f = brep_a.bitcast(F32)
        crep_v = crep_a.rearrange("p (n t) -> p n t", n=NS)
        ybf_v = ybf_a.rearrange("p (d t) -> p d t", d=DT)
        z_red = z_a.rearrange("p (n t) -> p t n", n=NS)
        i32_brep = brep_a.bitcast(I32)
        i32_h = h_a.bitcast(I32)
        i32_xs = xs_a.bitcast(I32)

        b.dma(lambda E: E.dma_start(out=A_sb[:, :], in_=din["A_log"][:, :]),
              writes=["A_sb"])
        b.op("s", lambda E: E.activation(out=h_a[:, 0:DT * NS], in_=A_sb[:, :], func=AF.Exp),
             reads=["A_sb"], writes=["h"])
        b.op("v", lambda E: E.tensor_scalar_mul(out=A_sb[:, :], in0=h_a[:, 0:DT * NS],
                                                scalar1=-1.0),
             reads=["h"], writes=["A_sb"])
        b.op("v", lambda E: E.memset(carry[:, :], 0.0), writes=["carry"])
        b.op("v", lambda E: E.memset(cst[:, 0:1], 1e-5), writes=["cst"])
        b.op("v", lambda E: E.memset(cst[:, 1:2], -1.0), writes=["cst"])
        b.op("v", lambda E: E.memset(gcol[:, :], 0.0), writes=["gcol"])
        b.op("v", lambda E: E.memset(zc[:, :], 0.0), writes=["zc"])
        b.op("v", lambda E: E.memset(gcol2[:, :], 0.0), writes=["gcol2"])
        b.dma(lambda E: E.dma_start(out=bin_sb[:, :], in_=din["bin"][:, :]),
              writes=["bin_sb"])
        b.dma(lambda E: E.dma_start(out=bdt_sb[:, :], in_=din["bdt"][:, :]),
              writes=["bdt_sb"])
        b.dma(lambda E: E.dma_start(out=dpv_sb[:, :], in_=din["Dp"][:, :]),
              writes=["dpv_sb"])
        b.dma(lambda E: E.dma_start(out=bout_sb[:, :], in_=din["bout"][:, :]),
              writes=["bout_sb"])
        b.dma(lambda E: E.dma_start(out=bx_sb[:, :], in_=din["bx"].unsqueeze(1)),
              writes=["bx_sb"])
        b.dma(lambda E: E.dma_start(out=idb[:, :], in_=din["ident_b"][:, :]), writes=["idb"])
        b.dma(lambda E: E.dma_start(out=idf[:, :], in_=din["ident_f"][:, :]), writes=["idf"])
        b.dma(lambda E: E.dma_start(out=tiny[0:1, 3:4], in_=din["clip_x"].unsqueeze(0)),
              writes=["tiny"])
        b.dma(lambda E: E.dma_start(out=tiny[0:1, 8:9], in_=din["clip_down"].unsqueeze(0)),
              writes=["tiny"])
        b.dma(lambda E: E.dma_start(out=h_a[:, 0:DT * 96], in_=din["Wx"][:, :]),
              writes=["h"])
        b.op("v", lambda E: E.tensor_copy(out=z_a[:, 0:DT * 96], in_=h_a[:, 0:DT * 96]),
             reads=["h"], writes=["z"])
        b.dma(lambda E: E.dma_start(out=sc_wx[:, :], in_=z_a[:, 0:DT * 96]),
              reads=["z"], writes=["sc_wx"], q="g")
        b.dma(lambda E: E.dma_start(out=h_a[0:R, 0:D], in_=din["Wdt"][:, :]), writes=["h"])
        b.op("v", lambda E: E.tensor_copy(out=wdt_sb[:, :], in_=h_a[0:R, 0:D]),
             reads=["h"], writes=["wdt_sb"])

        inv_h = 1.0 / float(H)
        p1sets = [
            (xtile, h_a[:, 0:H], h_a[:, H:2 * H], "h", lcol, "lcol",
             z_a[:, 0:H], "z", crep_a[:, 0:H], "crep"),
            (h_a[:, 5120:6144], h_a[:, 3072:4096], h_a[:, 4096:5120], "hB",
             lcol2, "lcolB", z_a[:, 2048:2048 + H], "zB",
             crep_a[:, 1024:1024 + H], "crepB"),
        ]
        for tt in range(NT):
            xt, sA, sB, hk2, lc, lck, zt, zk, crp, crk = p1sets[tt % 2]
            tsl = slice(tt * 128, (tt + 1) * 128)
            b.dma(lambda E, tsl=tsl, xt=xt: E.dma_start(out=xt, in_=din["x"][tsl, :]),
                  writes=[hk2])
            b.op("v", lambda E, lc=lc, xt=xt: E.reduce_sum(out=lc[:, 0:1], in_=xt, axis=AX),
                 reads=[hk2], writes=[lck])
            b.op("s", lambda E, sA=sA, xt=xt: E.activation(out=sA, in_=xt, func=AF.Square),
                 reads=[hk2], writes=[hk2])
            b.op("v", lambda E, lc=lc, sA=sA: E.reduce_sum(out=lc[:, 1:2], in_=sA, axis=AX),
                 reads=[hk2], writes=[lck])
            b.op("v", lambda E, lc=lc: E.tensor_scalar_mul(out=lc[:, 2:3], in0=lc[:, 0:1],
                                                           scalar1=inv_h),
                 reads=[lck], writes=[lck])
            b.op("v", lambda E, lc=lc: E.tensor_scalar_mul(out=lc[:, 3:4], in0=lc[:, 1:2],
                                                           scalar1=inv_h),
                 reads=[lck], writes=[lck])
            b.op("v", lambda E, lc=lc: E.scalar_tensor_tensor(out=lc[:, 4:5], in0=lc[:, 2:3],
                                                              scalar=lc[:, 2:3], in1=lc[:, 3:4],
                                                              op0=ALU.mult, op1=ALU.subtract),
                 reads=[lck], writes=[lck])
            b.op("s", lambda E, lc=lc: E.activation(out=lc[:, 5:6], in_=lc[:, 4:5],
                                                    func=AF.Sqrt,
                                                    scale=cst[:, 1:2], bias=cst[:, 0:1]),
                 reads=[lck, "cst"], writes=[lck])
            b.op("v", lambda E, lc=lc: E.reciprocal(out=lc[:, 6:7], in_=lc[:, 5:6]),
                 reads=[lck], writes=[lck])
            b.op("v", lambda E, lc=lc: E.scalar_tensor_tensor(out=lc[:, 7:8], in0=lc[:, 2:3],
                                                              scalar=-1.0, in1=lc[:, 6:7],
                                                              op0=ALU.mult, op1=ALU.mult),
                 reads=[lck], writes=[lck])
            b.op("s", lambda E, sB=sB, xt=xt, lc=lc: E.activation(
                out=sB, in_=xt, func=AF.Identity,
                scale=lc[:, 6:7], bias=lc[:, 7:8]),
                reads=[hk2, lck], writes=[hk2])
            b.op("v", lambda E, lc=lc, sB=sB: E.tensor_reduce(
                out=lc[:, 0:1], in_=sB, axis=AX,
                op=ALU.max, apply_absolute_value=True),
                reads=[hk2], writes=[lck])
            b.op("v", lambda E, lc=lc: E.tensor_max(out=gcol[:, :], in0=gcol[:, :],
                                                    in1=lc[:, 0:1]),
                 reads=[lck, "gcol"], writes=["gcol"])
            b.dma(lambda E, tsl=tsl, sB=sB: E.dma_start(out=sc_xn[tsl, :], in_=sB),
                  reads=[hk2], writes=["sc_xn.%d" % tt], q="g")
            b.op("v", lambda E, zt=zt, sB=sB: E.tensor_copy(out=zt, in_=sB),
                 reads=[hk2], writes=[zk])
            for k in range(HT):
                pb = ptb_pair[k % 2]
                pbk = "ptb%d" % (k % 2)
                b.op("t", lambda E, k=k, pb=pb, zt=zt: E.transpose(
                    pb[:, :], zt[:, k * 128:(k + 1) * 128], idb[:, :]),
                    reads=[zk, "idb"], writes=[pbk])
                b.op("s", lambda E, k=k, pb=pb, crp=crp: E.copy(
                    out=crp[:, k * 128:(k + 1) * 128], in_=pb[:, :]),
                    reads=[pbk], writes=[crk])
            b.dma(lambda E, tsl=tsl, crp=crp: E.dma_start(
                out=sc_xnT.rearrange("(k p) t -> p k t", p=128)[:, :, tsl],
                in_=crp[:, 0:H].rearrange("p (k f) -> p k f", k=HT)),
                reads=[crk], writes=["sc_xnT.%d" % tt], q="g")
        b.fork(["hB"], ["h"])
        b.fork(["lcolB"], ["lcol"])
        b.fork(["zB"], ["z"])
        b.fork(["crepB"], ["crep"])

        tinyrow = fga[0:1, 0:128]

        def allreduce_scalar(colbuf, colkey, cc_in, cc_out, clip_slot, s_slot, cc_name, sc_off):
            b.fork(["fga0"], ["tinyrow"])
            b.dma(lambda E: E.dma_start(out=sc_col.rearrange("(p o) -> p o", o=1),
                                        in_=colbuf[:, :]),
                  reads=[colkey], writes=["sc_col" + colkey])
            b.dma(lambda E: E.dma_start(out=tinyrow[:, :], in_=sc_col.unsqueeze(0)),
                  reads=["sc_col" + colkey], writes=["tinyrow"])
            b.op("v", lambda E: E.tensor_reduce(out=tiny[0:1, 0:1], in_=tinyrow[:, :],
                                                axis=AX, op=ALU.max),
                 reads=["tinyrow"], writes=["tiny"])
            b.dma(lambda E: E.dma_start(out=cc_in[:, :], in_=tiny[0:1, 0:1]),
                  reads=["tiny"], writes=[cc_name + "_in"])
            b.op("g", lambda E: E.collective_compute(
                "AllReduce", ALU.max, ins=[cc_in[:, :]], outs=[cc_out[:, :]],
                replica_groups=[[0, 1, 2, 3, 4, 5, 6, 7]]),
                reads=[cc_name + "_in"], writes=[cc_name + "_out"])
            b.dma(lambda E: E.dma_start(out=tiny[0:1, 1:2], in_=cc_out[:, :]),
                  reads=[cc_name + "_out"], writes=["tiny"])
            b.op("s", lambda E: E.activation(out=tiny[0:1, 4:5],
                                             in_=tiny[0:1, clip_slot:clip_slot + 1],
                                             func=AF.Sigmoid),
                 reads=["tiny"], writes=["tiny"])
            b.op("v", lambda E: E.tensor_mul(out=tiny[0:1, 5:6], in0=tiny[0:1, 4:5],
                                             in1=tiny[0:1, 1:2]),
                 reads=["tiny"], writes=["tiny"])
            b.op("v", lambda E: E.tensor_scalar_mul(out=tiny[0:1, 6:7], in0=tiny[0:1, 5:6],
                                                    scalar1=1.0 / QMAXI),
                 reads=["tiny"], writes=["tiny"])
            b.op("v", lambda E: E.reciprocal(out=tiny[0:1, 7:8], in_=tiny[0:1, 6:7]),
                 reads=["tiny"], writes=["tiny"])
            b.dma(lambda E: E.dma_start(out=sc_s.unsqueeze(0)[:, sc_off:sc_off + 2],
                                        in_=tiny[0:1, 6:8]),
                  reads=["tiny"], writes=["sc_s" + cc_name])
            b.dma(lambda E: E.dma_start(
                out=cols[:, s_slot:s_slot + 2],
                in_=sc_s.unsqueeze(0)[:, sc_off:sc_off + 2].broadcast_to([128, 2])),
                reads=["sc_s" + cc_name], writes=["cols"])

        allreduce_scalar(gcol, "gcol", cc1_in, cc1_out, 3, 0, "cc1", 0)

        def wquant(src, ntile, C, srow_sb, srowkey, dstT, dstkey):
            KB = C // 128
            for r in range(ntile):
                rsl = slice(r * 128, (r + 1) * 128)
                b.dma(lambda E, rsl=rsl, C=C, src=src: E.dma_start(out=xs_a[:, 0:C],
                                                                   in_=src[rsl, :]),
                      writes=["xs"])
                b.op("v", lambda E, C=C: E.tensor_reduce(out=lcol[:, 0:1], in_=xs_a[:, 0:C],
                                                         axis=AX, op=ALU.max,
                                                         apply_absolute_value=True),
                     reads=["xs"], writes=["lcol"])
                b.op("v", lambda E: E.reciprocal(out=lcol[:, 1:2], in_=lcol[:, 0:1]),
                     reads=["lcol"], writes=["lcol"])
                b.op("v", lambda E, r=r, srow_sb=srow_sb: E.tensor_scalar_mul(
                    out=srow_sb[:, r:r + 1], in0=lcol[:, 0:1], scalar1=1.0 / QMAXI),
                    reads=["lcol"], writes=[srowkey])
                b.op("v", lambda E: E.tensor_scalar_mul(out=lcol[:, 2:3], in0=lcol[:, 1:2],
                                                        scalar1=QMAXI),
                     reads=["lcol"], writes=["lcol"])
                b.op("s", lambda E, C=C: E.activation(out=h_a[:, 0:C], in_=xs_a[:, 0:C],
                                                      func=AF.Identity, scale=lcol[:, 2:3]),
                     reads=["xs", "lcol"], writes=["h"])
                b.op("v", lambda E, C=C: E.tensor_copy(out=i32_brep[:, 0:C], in_=h_a[:, 0:C]),
                     reads=["h"], writes=["brep"])
                b.op("v", lambda E, C=C: E.tensor_copy(out=z_a[:, 0:C], in_=i32_brep[:, 0:C]),
                     reads=["brep"], writes=["z"])
                for k in range(KB):
                    pb = ptb_pair[k % 2]
                    pbk = "ptb%d" % (k % 2)
                    b.op("t", lambda E, k=k, pb=pb: E.transpose(
                        pb[:, :], z_a[:, k * 128:(k + 1) * 128], idb[:, :]),
                        reads=["z", "idb"], writes=[pbk])
                    b.op("s", lambda E, k=k, pb=pb: E.copy(
                        out=crep_a[:, k * 128:(k + 1) * 128], in_=pb[:, :]),
                        reads=[pbk], writes=["crep"])
                b.dma(lambda E, r=r, KB=KB, C=C, dstT=dstT: E.dma_start(
                    out=dstT[r, :, :, :],
                    in_=crep_a[:, 0:C].rearrange("p (k f) -> p k f", k=KB)),
                    reads=["crep"], writes=[dstkey], q="g")

        wquant(din["w_gate"], IT, H, srg, "srg", sc_wgT, "sc_wgT")
        wquant(din["w_up"], IT, H, sru, "sru", sc_wuT, "sc_wuT")
        wquant(din["w_down"], HT, II, srd, "srd", sc_wdT, "sc_wdT")

        for r in range(HT):
            rsl = slice(r * 128, (r + 1) * 128)
            b.dma(lambda E, rsl=rsl: E.dma_start(out=xs_a[:, 0:4096], in_=din["Win"][rsl, :]),
                  writes=["xs"])
            b.op("v", lambda E: E.tensor_copy(out=z_a[:, 0:4096], in_=xs_a[:, 0:4096]),
                 reads=["xs"], writes=["z"])
            b.dma(lambda E, r=r: E.dma_start(
                out=sc_winb[:, :, r, :].transpose([1, 0, 2]),
                in_=z_a[:, 0:4096].rearrange("p (ct c) -> p ct c", ct=32)),
                reads=["z"], writes=["sc_winb"], q="g")
        for r in range(DT):
            rsl = slice(r * 128, (r + 1) * 128)
            b.dma(lambda E, rsl=rsl: E.dma_start(out=xs_a[:, 0:H], in_=din["Wout"][rsl, :]),
                  writes=["xs"])
            b.op("v", lambda E: E.tensor_copy(out=z_a[:, 0:H], in_=xs_a[:, 0:H]),
                 reads=["xs"], writes=["z"])
            b.dma(lambda E, r=r: E.dma_start(
                out=sc_woutb[:, :, r, :].transpose([1, 0, 2]),
                in_=z_a[:, 0:H].rearrange("p (ht c) -> p ht c", ht=HT)),
                reads=["z"], writes=["sc_woutb"], q="g")

        b.op("v", lambda E: E.tensor_scalar(out=sg[:, :], in0=srg[:, :], scalar1=cols[:, 0:1],
                                            scalar2=None, op0=ALU.mult),
             reads=["srg", "cols"], writes=["sg"])
        b.op("v", lambda E: E.tensor_scalar(out=su[:, :], in0=sru[:, :], scalar1=cols[:, 0:1],
                                            scalar2=None, op0=ALU.mult),
             reads=["sru", "cols"], writes=["su"])

        b.fork(["brep"], ["brepB"])
        b.fork(["z"], ["zB"])
        b.fork(["crep"], ["crepB"])
        b.fork(["h"], ["hB"])
        p1csets = [
            (xtile, "h", brep_f[:, 0:H], "brep", i32_h[:, 0:H],
             z_a[:, 0:H], "z", crep_a[:, 0:H], "crep"),
            (h_a[:, 5120:6144], "hB", brep_f[:, 1024:2048], "brepB",
             i32_h[:, 3072:3072 + H], z_a[:, 2048:2048 + H], "zB",
             crep_a[:, 1024:1024 + H], "crepB"),
        ]
        for tt in range(NT):
            xt, hk2, brf, brk, i32t, zt, zk, crp, crk = p1csets[tt % 2]
            tsl = slice(tt * 128, (tt + 1) * 128)
            b.dma(lambda E, tsl=tsl, tt=tt, xt=xt: E.dma_start(out=xt, in_=sc_xn[tsl, :]),
                  reads=["sc_xn.%d" % tt], writes=[hk2])
            b.op("s", lambda E, brf=brf, xt=xt: E.activation(
                out=brf, in_=xt, func=AF.Identity, scale=cols[:, 1:2]),
                reads=[hk2, "cols"], writes=[brk])
            b.op("v", lambda E, i32t=i32t, brf=brf: E.tensor_copy(out=i32t, in_=brf),
                 reads=[brk], writes=[hk2])
            b.op("v", lambda E, zt=zt, i32t=i32t: E.tensor_scalar(
                out=zt, in0=i32t,
                scalar1=-(QMAXI + 1.0), scalar2=QMAXI,
                op0=ALU.max, op1=ALU.min),
                reads=[hk2], writes=[zk])
            for k in range(HT):
                pb = ptb_pair[k % 2]
                pbk = "ptb%d" % (k % 2)
                b.op("t", lambda E, k=k, pb=pb, zt=zt: E.transpose(
                    pb[:, :], zt[:, k * 128:(k + 1) * 128], idb[:, :]),
                    reads=[zk, "idb"], writes=[pbk])
                b.op("s", lambda E, k=k, pb=pb, crp=crp: E.copy(
                    out=crp[:, k * 128:(k + 1) * 128], in_=pb[:, :]),
                    reads=[pbk], writes=[crk])
            b.dma(lambda E, tsl=tsl, crp=crp: E.dma_start(
                out=sc_xqT.rearrange("(k p) t -> p k t", p=128)[:, :, tsl],
                in_=crp[:, 0:H].rearrange("p (k f) -> p k f", k=HT)),
                reads=[crk], writes=["sc_xqT.%d" % tt], q="g")
        b.fork(["brepB"], ["brep"])
        b.fork(["zB"], ["z"])
        b.fork(["crepB"], ["crep"])
        b.fork(["hB"], ["h"])

        F1_STEPS = CH * (IT + 1)
        f1L = {"i": 0}
        f1q = []

        def f1_load():
            i = f1L["i"]
            if i >= F1_STEPS:
                return
            fc = i // (IT + 1)
            st = i % (IT + 1)
            fsl = slice(fc * TC, (fc + 1) * TC)
            if st == 0:
                if f1q:
                    return
                f1L["i"] += 1
                tts = ["sc_xqT.%d" % (4 * fc + j) for j in range(4)]
                b.dma(lambda E, fsl=fsl: E.dma_start(
                    out=xqv[:, :, :],
                    in_=sc_xqT.rearrange("(k p) t -> p k t", p=128)[:, :, fsl]),
                    reads=tts, writes=["zxq"], q="s")
                return
            f1L["i"] += 1
            it = st - 1
            wfg, gk = grabB()
            wfu, uk2 = grabB()
            b.dma(lambda E, it=it, wfg=wfg: E.dma_start(out=wfg[:, :, :],
                                                        in_=sc_wgT[it, :, :, :]),
                  reads=["sc_wgT"], writes=[gk], q="s")
            b.dma(lambda E, it=it, wfu=wfu: E.dma_start(out=wfu[:, :, :],
                                                        in_=sc_wuT[it, :, :, :]),
                  reads=["sc_wuT"], writes=[uk2], q="s")
            f1q.append((it, fsl, fc, wfg, gk, wfu, uk2))

        fgap = [fga, fgb, fga2, fgb2]
        f1cnt = {"i": 0}

        def f1_mm():
            if not f1q:
                return
            it, fsl, fc, wfg, gk, wfu, uk2 = f1q.pop(0)
            fp2 = f1cnt["i"] % 2
            f1cnt["i"] += 1
            ga, gb = fgap[2 * fp2], fgap[2 * fp2 + 1]
            gak, gbk = "fga%d" % fp2, "fgb%d" % fp2
            ppg, ppgk = cl_bank()
            ppu, ppuk = cl_bank()
            for hk in range(HT):
                b.op("t", lambda E, hk=hk, ppg=ppg, wfg=wfg: E.matmul(
                    ppg[:, :], wfg[:, hk, :], xqv[:, hk, :],
                    start=(hk == 0), stop=(hk == HT - 1)),
                    reads=[gk, "zxq"], writes=[ppgk])
            for hk in range(HT):
                b.op("t", lambda E, hk=hk, ppu=ppu, wfu=wfu: E.matmul(
                    ppu[:, :], wfu[:, hk, :], xqv[:, hk, :],
                    start=(hk == 0), stop=(hk == HT - 1)),
                    reads=[uk2, "zxq"], writes=[ppuk])

            def tail(it=it, fsl=fsl, fc=fc, ppg=ppg, ppgk=ppgk, ppu=ppu, ppuk=ppuk,
                     ga=ga, gb=gb, gak=gak, gbk=gbk):
                b.op("s", lambda E: E.activation(out=ga[:, :], in_=ppg[:, :],
                                                 func=AF.Silu, scale=sg[:, it:it + 1]),
                     reads=[ppgk, "sg"], writes=[gak])
                b.op("s", lambda E: E.activation(out=gb[:, :], in_=ppu[:, :],
                                                 func=AF.Identity, scale=su[:, it:it + 1]),
                     reads=[ppuk, "su"], writes=[gbk])
                b.op("v", lambda E: E.tensor_mul(out=ga[:, :], in0=ga[:, :],
                                                 in1=gb[:, :]),
                     reads=[gak, gbk], writes=[gak])
                b.op("v", lambda E: E.tensor_reduce(out=lcol[:, 0:1], in_=ga[:, :],
                                                    axis=AX, op=ALU.max,
                                                    apply_absolute_value=True),
                     reads=[gak, "lcol"], writes=["lcol"])
                b.op("v", lambda E: E.tensor_max(out=gcol2[:, :], in0=gcol2[:, :],
                                                 in1=lcol[:, 0:1]),
                     reads=["lcol", "gcol2"], writes=["gcol2"])
                b.dma(lambda E: E.dma_start(
                    out=sc_mid[it * 128:(it + 1) * 128, fsl], in_=ga[:, :]),
                    reads=[gak], writes=["sc_mid.%d.%d" % (fc, it)], q="g")

            pend.append((frozenset([ppgk, ppuk]), tail))

        pst = {"wsl": 0, "pp": 0, "rot": 0}
        cl_banks = [p0, p1, pA, pB]
        cl_keys = ["p0", "p1", "pA", "pB"]
        pend = []
        pend_late = []

        def flush_pend():
            for _, f in pend:
                f()
            del pend[:]

        def flush_for(bk):
            while any(bk in banks for banks, _ in pend):
                banks, f = pend.pop(0)
                f()

        def cl_bank():
            r = pst["rot"]
            pst["rot"] = (r + 1) % 4
            flush_for(cl_keys[r])
            return cl_banks[r], cl_keys[r]

        def flush_one():
            if pend:
                banks, f = pend.pop(0)
                f()

        def flush_late():
            for f in pend_late:
                f()
            del pend_late[:]

        def emit_bc(c2):
            csl2 = slice(c2 * TC, (c2 + 1) * TC)
            for qq in range(4):
                b.dma(lambda E, qq=qq, csl2=csl2: E.dma_start(
                    out=brep_v[:, 4 * qq:4 * qq + 4, :],
                    in_=sc_bcB[4 * qq:4 * qq + 4, csl2].unsqueeze(0)
                    .broadcast_to([128, 4, TC])),
                    reads=["sc_bcB.%d" % c2], writes=["brep"], q="g")
            b.dma(lambda E, csl2=csl2: E.dma_start(
                out=crep_v[:, :, :],
                in_=sc_bcC[:, csl2].unsqueeze(0).broadcast_to([128, NS, TC])),
                reads=["sc_bcC.%d" % c2], writes=["crep"], q="g")

        xsg = {"c": 0, "ct": 0}
        xsq = []

        def xs_load():
            c2 = xsg["c"]
            if c2 >= CH:
                return
            ct = xsg["ct"]
            csl2 = slice(c2 * TC, (c2 + 1) * TC)
            xslot = c2 % 2
            if ct == 0:
                tts = ["sc_xnT.%d" % (4 * c2 + j) for j in range(4)]
                b.dma(lambda E, csl2=csl2, xslot=xslot: E.dma_start(
                    out=xnt_t[xslot][:, :, :],
                    in_=sc_xnT.rearrange("(k p) t -> p k t", p=128)[:, :, csl2]),
                    reads=tts, writes=[xnt_keys[xslot]])
            wv, wk = grabA()
            b.dma(lambda E, ct=ct, wv=wv: E.dma_start(
                out=wv[:, :, :], in_=sc_winb[ct, :, :, :]),
                reads=["sc_winb"], writes=[wk])
            wxp = ct % 4
            b.dma(lambda E, ct=ct, wxp=wxp: E.dma_start(
                out=wx2[:, wxp * 96:(wxp + 1) * 96],
                in_=sc_wx[:, ct * 96:(ct + 1) * 96]),
                reads=["sc_wx"], writes=["wx%d" % wxp])
            xsq.append((wv, wk, ct, csl2, c2, xslot))
            if ct == DT - 1:
                xsg["c"] += 1
                xsg["ct"] = 0
            else:
                xsg["ct"] += 1

        def xs_mm():
            if not xsq:
                return
            wv, wk, ct, csl2, c2, xslot = xsq.pop(0)
            pp, ppk = cl_bank()
            for hk in range(HT):
                b.op("t", lambda E, hk=hk, wv=wv, pp=pp, xslot=xslot: E.matmul(
                    pp[:, :], wv[:, hk, :], xnt_t[xslot][:, hk, :],
                    start=(hk == 0), stop=(hk == HT - 1)),
                    reads=[wk, xnt_keys[xslot]], writes=[ppk])

            def emit_p2(ct2):
                wxp = ct2 % 4
                xb, xbk = (xsb0, "xsb0") if ct2 % 2 == 0 else (xsb1, "xsb1")
                b.op("t", lambda E, wxp=wxp, ct2=ct2, xb=xb: E.matmul(
                    p2[:, :], wx2[:, wxp * 96:(wxp + 1) * 96], xb[:, :],
                    start=(ct2 == 0), stop=(ct2 == DT - 1)),
                    reads=["wx%d" % wxp, xbk], writes=["p2"])

            def tail(ct=ct, csl2=csl2, c2=c2, pp=pp, ppk=ppk):
                st2 = ct % 2
                b.op("s", lambda E: E.activation(
                    out=xsst[st2], in_=pp[:, :], func=AF.Silu,
                    bias=bin_sb[:, ct:ct + 1]),
                    reads=[ppk, "bin_sb"], writes=["xsst%d" % st2])
                b.dma(lambda E: E.dma_start(
                    out=sc_xs[ct * 128:(ct + 1) * 128, csl2], in_=xsst[st2]),
                    reads=["xsst%d" % st2], writes=["sc_xs.%d.%d" % (c2, ct)], q="g")
                xb, xbk = (xsb0, "xsb0") if ct % 2 == 0 else (xsb1, "xsb1")
                b.op("s", lambda E, xb=xb: E.activation(
                    out=xb[:, :], in_=pp[:, :], func=AF.Silu,
                    bias=bin_sb[:, ct:ct + 1]),
                    reads=[ppk, "bin_sb"], writes=[xbk])
                if ct > 0:
                    emit_p2(ct - 1)

            def late(csl2=csl2, c2=c2):
                emit_p2(DT - 1)
                b.op("s", lambda E: E.activation(out=xp[:, :], in_=p2[:, :],
                                                 func=AF.Identity, bias=bx_sb[:, 0:1]),
                     reads=["p2", "bx_sb"], writes=["xp"])
                b.op("s", lambda E: E.copy(out=dpb[:, :], in_=xp[0:R, :]),
                     reads=["xp"], writes=["dpb"])
                b.fork(["xsst0", "xsst1"], ["cbf"])
                b.op("s", lambda E: E.copy(out=cbf[:, :], in_=xp[64:96, :]),
                     reads=["xp"], writes=["cbf"])
                b.dma(lambda E: E.dma_start(out=sc_bcB[:, csl2], in_=cbf[0:NS, :]),
                      reads=["cbf"], writes=["sc_bcB.%d" % c2], q="g")
                b.dma(lambda E: E.dma_start(out=sc_bcC[:, csl2],
                                            in_=cbf[NS:2 * NS, :]),
                      reads=["cbf"], writes=["sc_bcC.%d" % c2], q="g")
                b.fork(["cbf"], ["xsst0", "xsst1"])

            pend.append((frozenset([ppk]), tail))
            if ct == DT - 1:
                pend_late.append(late)

        wog = {"c": -1, "ht": 0}
        woq = []

        def wout_load():
            c1 = wog["c"]
            if c1 < 0 or wog["ht"] >= HT:
                return
            ht = wog["ht"]
            wog["ht"] += 1
            wa, ak = grabB()
            wb2, bk2 = grabB()
            b.dma(lambda E, ht=ht, wa=wa: E.dma_start(
                out=wa[:, :, :], in_=sc_woutb[ht, :, 0:8, :]),
                reads=["sc_woutb"], writes=[ak], q="s")
            b.dma(lambda E, ht=ht, wb2=wb2: E.dma_start(
                out=wb2[:, :, :], in_=sc_woutb[ht, :, 8:16, :]),
                reads=["sc_woutb"], writes=[bk2], q="s")
            woq.append((wa, ak, wb2, bk2, ht, c1))

        def wout_mm():
            if not woq:
                return
            wa, ak, wb2, bk2, ht, c1 = woq.pop(0)
            yv = ybf_t[c1 % 2]
            yk = ybf_keys[c1 % 2]
            csl1 = slice(c1 * TC, (c1 + 1) * TC)
            pp, ppk = cl_bank()
            for dk in range(DT):
                w_ap = wa[:, dk, :] if dk < 8 else wb2[:, dk - 8, :]
                wk = ak if dk < 8 else bk2
                b.op("t", lambda E, dk=dk, w_ap=w_ap, wk=wk, yv=yv, pp=pp: E.matmul(
                    pp[:, :], w_ap, yv[:, dk, :],
                    start=(dk == 0), stop=(dk == DT - 1)),
                    reads=[wk, yk], writes=[ppk])

            def tail(ht=ht, csl1=csl1, c1=c1, pp=pp, ppk=ppk):
                par = ht % 2
                b.op("s", lambda E: E.activation(
                    out=stgpair[par][:, :], in_=pp[:, :], func=AF.Identity,
                    bias=bout_sb[:, ht:ht + 1]),
                    reads=[ppk, "bout_sb"], writes=[stgkey[par]])
                b.dma(lambda E: E.dma_start(
                    out=sc_part[ht * 128:(ht + 1) * 128, csl1], in_=stgpair[par][:, :]),
                    reads=[stgkey[par]], writes=["sc_part.%d.%d" % (c1, ht)], q="g")

            pend.append((frozenset([ppk]), tail))

        gzq = []

        def gz_load(dts):
            for d2 in dts:
                ct = DT + d2
                wv, wk = grabA()
                b.dma(lambda E, ct=ct, wv=wv: E.dma_start(
                    out=wv[:, :, :], in_=sc_winb[ct, :, :, :]),
                    reads=["sc_winb"], writes=[wk])
                gzq.append((wv, wk, d2))

        def gz_mm(n2, xv, xk):
            got = []
            for _ in range(n2):
                wv, wk, d2 = gzq.pop(0)
                pp, ppk = cl_bank()
                got.append((d2, pp, ppk))
                for hk in range(HT):
                    b.op("t", lambda E, hk=hk, pp=pp, wv=wv, xv=xv: E.matmul(
                        pp[:, :], wv[:, hk, :], xv[:, hk, :],
                        start=(hk == 0), stop=(hk == HT - 1)),
                        reads=[wk, xk], writes=[ppk])

            def tail(got=got):
                for d2, pp, ppk in got:
                    p2r = d2 % 2
                    ct = DT + d2
                    b.op("s", lambda E, ct=ct, p2r=p2r, pp=pp, ppk=ppk: E.activation(
                        out=gzpair[p2r][:, :], in_=pp[:, :], func=AF.Silu,
                        bias=bin_sb[:, ct:ct + 1]),
                        reads=[ppk, "bin_sb"], writes=[gzkey[p2r]])

            pend.append((frozenset(k for _, _, k in got), tail))

        p3pair = [p3, ptfA]
        p3keys = ["p3", "p3b"]

        def delta_chain(dt):
            par = dt % 2
            pd, pdk = p3pair[par], p3keys[par]
            b.op("t", lambda E, dt=dt, pd=pd: E.matmul(
                pd[:, :], wdt_sb[:, dt * 128:(dt + 1) * 128], dpb[:, :],
                start=True, stop=True),
                reads=["wdt_sb", "dpb"], writes=[pdk])
            b.op("s", lambda E, dt=dt, pd=pd: E.activation(
                out=dA0[:, :], in_=pd[:, :], func=AF.Exp,
                bias=bdt_sb[:, dt:dt + 1]),
                reads=[pdk, "bdt_sb"], writes=["dA0"])
            b.op("s", lambda E, par=par: E.activation(
                out=dpair[par][:, :], in_=dA0[:, :], func=AF.Ln, bias=1.0),
                reads=["dA0"], writes=[dkey[par]])

        b.fork(["ptb0"], ["pA"])
        b.fork(["ptb1"], ["pB"])
        for _ in range(DT):
            xs_load()
            xs_mm()
            flush_pend()
        flush_late()
        emit_bc(0)

        for c in range(CH):
            csl = slice(c * TC, (c + 1) * TC)
            xslot = c % 2
            xk = xnt_keys[xslot]
            xv = xnt_t[xslot]
            ybf_cur = ybf_t[c % 2]
            ybk = ybf_keys[c % 2]
            wog["c"] = c - 1
            wog["ht"] = 0
            flush_pend()
            if c == 0:
                gz_load((0, 1))
            gz_mm(2, xv, xk)
            if c == 0:
                delta_chain(0)
            flush_pend()

            for dt in range(DT):
                par = dt % 2
                if dt % 2 == 1 and dt + 2 < DT:
                    gz_load((dt + 1, dt + 2))
                if dt == DT - 1 and c + 1 < CH:
                    gz_load((0, 1))
                f1_load()
                if dt % 2 == 0:
                    f1_load()
                if xsg["c"] == c + 1:
                    xs_load()
                if dt < 7 and xsg["c"] == c + 1:
                    xs_load()
                if dt % 2 == 1:
                    wout_load()
                if dt == 0 and c == 0:
                    for d0 in (0, 1):
                        b.dma(lambda E, d0=d0, csl=csl, c=c: E.dma_start(
                            out=xsld[d0], in_=sc_xs[d0 * 128:(d0 + 1) * 128, csl]),
                            reads=["sc_xs.%d.%d" % (c, d0)], writes=["xsld%d" % d0])
                b.op("v", lambda E, dt=dt, par=par: E.tensor_mul(
                    out=dxb[:, :], in0=dpair[par][:, :], in1=xsld[dt % 2]),
                    reads=[dkey[par], "xsld%d" % (dt % 2)], writes=["dxb"])
                for g in range(4):
                    gs = g % 2
                    da, dak, dav = dA4p[gs], dA4keys[gs], dA4v[gs]
                    base = dt * NS + 4 * g
                    for j in range(4):
                        n = 4 * g + j
                        b.op("s", lambda E, n=n, j=j, par=par, da=da: E.activation(
                            out=da[:, j * TC:(j + 1) * TC], in_=dpair[par][:, :],
                            func=AF.Exp,
                            scale=A_sb[:, dt * NS + n:dt * NS + n + 1]),
                            reads=[dkey[par], "A_sb"], writes=[dak])
                    b.op("v", lambda E, g=g: E.tensor_mul(
                        out=u4v[:, :, :],
                        in0=dxb[:, :].unsqueeze(1).broadcast_to([128, 4, TC]),
                        in1=brep_v[:, 4 * g:4 * g + 4, :]),
                        reads=["dxb", "brep"], writes=["u4"])
                    b.op("v", lambda E, dav=dav, base=base: E.tensor_mul(
                        out=lcol[:, 4:8].unsqueeze(2), in0=dav[:, :, 0:1],
                        in1=carry[:, base:base + 4].unsqueeze(2)),
                        reads=[dak, "carry", "lcol"], writes=["lcol"])
                    b.op("v", lambda E, dav=dav: E.tensor_scalar_mul(
                        out=dav[:, :, 0:1], in0=dav[:, :, 0:1], scalar1=0.0),
                        reads=[dak], writes=[dak])
                    b.op("v", lambda E: E.tensor_add(
                        out=u4v[:, :, 0:1], in0=u4v[:, :, 0:1],
                        in1=lcol[:, 4:8].unsqueeze(2)),
                        reads=["u4", "lcol"], writes=["u4"])
                    b.op("v", lambda E, g=g, da=da: E.tensor_tensor_scan(
                        out=h_a[:, 4 * g * TC:(4 * g + 4) * TC], data0=da[:, :],
                        data1=u4flat[:, :], initial=zc[:, 0:1],
                        op0=ALU.mult, op1=ALU.add),
                        reads=[dak, "u4", "zc"], writes=["h"])
                    flush_one()
                    if g == 1:
                        flush_one()
                if dt + 1 < DT:
                    delta_chain(dt + 1)
                flush_pend()
                if dt == DT - 2:
                    flush_late()
                    if c + 1 < CH:
                        delta_chain(0)
                b.op("s", lambda E, dt=dt: E.copy(
                    out=carry[:, dt * NS:(dt + 1) * NS],
                    in_=h_v[:, :, TC - 1]),
                    reads=["h"], writes=["carry"])
                b.op("v", lambda E: E.tensor_mul(out=h_a[:, :], in0=h_a[:, :],
                                                 in1=crep_a[:, :]),
                     reads=["h", "crep"], writes=["h"])
                b.op("v", lambda E: E.tensor_add(out=h_a[:, 0:4096], in0=h_a[:, 0:4096],
                                                 in1=h_a[:, 4096:8192]),
                     reads=["h"], writes=["h"])
                b.op("v", lambda E: E.tensor_add(out=h_a[:, 0:2048], in0=h_a[:, 0:2048],
                                                 in1=h_a[:, 2048:4096]),
                     reads=["h"], writes=["h"])
                b.op("v", lambda E: E.tensor_add(out=h_a[:, 0:1024], in0=h_a[:, 0:1024],
                                                 in1=h_a[:, 1024:2048]),
                     reads=["h"], writes=["h"])
                b.op("v", lambda E: E.tensor_add(out=dxb[:, :], in0=h_a[:, 0:TC],
                                                 in1=h_a[:, TC:2 * TC]),
                     reads=["h"], writes=["dxb"])
                b.op("v", lambda E, dt=dt: E.scalar_tensor_tensor(
                    out=u0[:, :], in0=xsld[dt % 2], scalar=dpv_sb[:, dt:dt + 1],
                    in1=dxb[:, :], op0=ALU.mult, op1=ALU.add),
                    reads=["xsld%d" % (dt % 2), "dpv_sb", "dxb"], writes=["u0"])
                b.op("v", lambda E, dt=dt, par=par, ybf_cur=ybf_cur: E.tensor_mul(
                    out=ybf_cur[:, dt, :], in0=u0[:, :], in1=gzpair[par][:, :]),
                    reads=["u0", gzkey[par]], writes=[ybk])
                if dt == DT - 1 and c + 1 < CH:
                    emit_bc(c + 1)
                    for d0 in (0, 1):
                        b.dma(lambda E, d0=d0, c=c: E.dma_start(
                            out=xsld[d0],
                            in_=sc_xs[d0 * 128:(d0 + 1) * 128,
                                      (c + 1) * TC:(c + 2) * TC]),
                            reads=["sc_xs.%d.%d" % (c + 1, d0)],
                            writes=["xsld%d" % d0])
                if dt + 2 < DT:
                    b.dma(lambda E, dt=dt, csl=csl, c=c: E.dma_start(
                        out=xsld[dt % 2],
                        in_=sc_xs[(dt + 2) * 128:(dt + 3) * 128, csl]),
                        reads=["sc_xs.%d.%d" % (c, dt + 2)], writes=["xsld%d" % (dt % 2)])
                if dt % 2 == 1 and dt + 2 < DT:
                    gz_mm(2, xv, xk)
                f1_mm()
                if dt % 2 == 0:
                    f1_mm()
                xs_mm()
                xs_mm()
                if dt % 2 == 1:
                    wout_mm()

        wog["c"] = CH - 1
        wog["ht"] = 0
        for _ in range(HT):
            wout_load()
            wout_mm()

        def f2_loads(c2):
            csl2 = slice(c2 * TC, (c2 + 1) * TC)
            midkeys = ["sc_mid.%d.%d" % (c2, it) for it in range(IT)]
            b.dma(lambda E, csl2=csl2: E.dma_start(
                out=xs_v[:, :, :],
                in_=sc_mid.rearrange("(k p) t -> p k t", p=128)[:, 0:16, csl2]),
                reads=midkeys[0:16], writes=["xs"], q="s")
            b.dma(lambda E, csl2=csl2: E.dma_start(
                out=brep_f[:, 0:6 * TC].rearrange("p (k t) -> p k t", k=6),
                in_=sc_mid.rearrange("(k p) t -> p k t", p=128)[:, 16:22, csl2]),
                reads=midkeys[16:22], writes=["brep"], q="s")

        b.fork(["xsld0", "xsld1", "xnt1", "ybf1", "xsst0", "xsst1"], ["xs"])
        f2_loads(0)

        while f1L["i"] < F1_STEPS or f1q:
            f1_load()
            f1_mm()
            flush_pend()
        flush_pend()

        allreduce_scalar(gcol2, "gcol2", cc2_in, cc2_out, 8, 2, "cc2", 2)
        b.op("v", lambda E: E.tensor_scalar(out=sd[:, :], in0=srd[:, :], scalar1=cols[:, 2:3],
                                            scalar2=None, op0=ALU.mult),
             reads=["srd", "cols"], writes=["sd"])

        b.fork(["p3b"], ["ptf0"])
        b.fork(["dA4_0", "dA4_1"], ["sum.%d" % ht for ht in range(HT)])
        b.fork(["u4", "zxq"], ["z"])
        b.fork(["wA0", "wA1"], ["wstr0"])
        b.fork(["wA2", "wA3"], ["wstr1"])
        b.fork(["ybf0"], ["xtile", "ost"])
        b.fork(["wB0"], ["wd1a"])
        b.fork(["wB1"], ["wd1b"])
        b.fork(["wB2"], ["wd1c"])
        z_q = z_a.rearrange("p (k t) -> p k t", k=NS)
        c_q = crep_a.rearrange("p (k t) -> p k t", k=NS)
        wd_a = wstr0[:, :].rearrange("p (k f) -> p k f", k=DT)
        wd_b = wstr1[:, 0:768].rearrange("p (k f) -> p k f", k=6)
        ybf_f32 = ybf_a.bitcast(F32)
        xtile2 = ybf_f32[:, 0:1024]
        ost = ybf_f32[:, 1024:2048]

        def f2_scale16(c2):
            b.op("s", lambda E: E.activation(out=h_a[:, :], in_=xs_a[:, :],
                                             func=AF.Identity, scale=cols[:, 3:4]),
                 reads=["xs", "cols"], writes=["h"])

        for c in range(CH):
            csl = slice(c * TC, (c + 1) * TC)
            if c == 0:
                f2_scale16(0)
            b.op("v", lambda E: E.tensor_copy(out=i32_xs[:, :], in_=h_a[:, :]),
                 reads=["h"], writes=["xs"])
            b.op("v", lambda E: E.tensor_scalar(out=z_a[:, :], in0=i32_xs[:, :],
                                                scalar1=-(QMAXI + 1.0), scalar2=QMAXI,
                                                op0=ALU.max, op1=ALU.min),
                 reads=["xs"], writes=["z"])
            b.op("s", lambda E: E.activation(out=h_a[:, 0:6 * TC], in_=brep_f[:, 0:6 * TC],
                                             func=AF.Identity, scale=cols[:, 3:4]),
                 reads=["brep", "cols"], writes=["h"])
            b.op("v", lambda E: E.tensor_copy(out=i32_brep[:, 0:6 * TC], in_=h_a[:, 0:6 * TC]),
                 reads=["h"], writes=["brep"])
            b.op("v", lambda E: E.tensor_scalar(out=crep_a[:, 0:6 * TC],
                                                in0=i32_brep[:, 0:6 * TC],
                                                scalar1=-(QMAXI + 1.0), scalar2=QMAXI,
                                                op0=ALU.max, op1=ALU.min),
                 reads=["brep"], writes=["crep"])

            if c + 1 < CH:
                f2_loads(c + 1)
                f2_scale16(c + 1)

            def wd_load(ht):
                if ht % 2 == 0:
                    b.dma(lambda E, ht=ht: E.dma_start(
                        out=wd_a[:, :, :], in_=sc_wdT[ht, :, 0:16, :]),
                        reads=["sc_wdT"], writes=["wstr0"])
                    b.dma(lambda E, ht=ht: E.dma_start(
                        out=wd_b[:, :, :], in_=sc_wdT[ht, :, 16:22, :]),
                        reads=["sc_wdT"], writes=["wstr1"])
                else:
                    b.dma(lambda E, ht=ht: E.dma_start(
                        out=wpB[0][:, :, :], in_=sc_wdT[ht, :, 0:8, :]),
                        reads=["sc_wdT"], writes=["wd1a"])
                    b.dma(lambda E, ht=ht: E.dma_start(
                        out=wpB[1][:, :, :], in_=sc_wdT[ht, :, 8:16, :]),
                        reads=["sc_wdT"], writes=["wd1b"])
                    b.dma(lambda E, ht=ht: E.dma_start(
                        out=wpB[2][:, 0:6, :], in_=sc_wdT[ht, :, 16:22, :]),
                        reads=["sc_wdT"], writes=["wd1c"])

            def wd_tile(ht, ik):
                if ht % 2 == 0:
                    if ik < 16:
                        return wd_a[:, ik, :], "wstr0"
                    return wd_b[:, ik - 16, :], "wstr1"
                if ik < 8:
                    return wpB[0][:, ik, :], "wd1a"
                if ik < 16:
                    return wpB[1][:, ik - 8, :], "wd1b"
                return wpB[2][:, ik - 16, :], "wd1c"

            wd_load(0)
            for ht in range(HT):
                par = ht % 2
                if ht + 1 < HT:
                    wd_load(ht + 1)
                b.dma(lambda E, ht=ht, csl=csl, par=par: E.dma_start(
                    out=stgpair[par][:, :], in_=sc_part[ht * 128:(ht + 1) * 128, csl]),
                    reads=["sc_part.%d.%d" % (c, ht)], writes=[stgkey[par]], q="y")
                pk = "p%d" % par
                for ik in range(IT):
                    w_ap, rk = wd_tile(ht, ik)
                    rhs_ap, qk = (z_q[:, ik, :], "z") if ik < 16 else                         (c_q[:, ik - 16, :], "crep")
                    b.op("t", lambda E, w_ap=w_ap, rhs_ap=rhs_ap, ik=ik, par=par: E.matmul(
                        ppair[par][:, :], w_ap, rhs_ap,
                        start=(ik == 0), stop=(ik == IT - 1)),
                        reads=[rk, qk], writes=[pk])
                sb_out = dA4a if ht < 4 else dA4b
                b.op("v", lambda E, ht=ht, par=par, sb_out=sb_out: E.scalar_tensor_tensor(
                    out=sb_out[:, (ht % 4) * TC:(ht % 4 + 1) * TC],
                    in0=ppair[par][:, :], scalar=sd[:, ht:ht + 1], in1=stgpair[par][:, :],
                    op0=ALU.mult, op1=ALU.add),
                    reads=[pk, "sd", stgkey[par]], writes=["sum.%d" % ht])

            for tbl in range(4):
                tb = 4 * c + tbl
                tsl = slice(tb * 128, (tb + 1) * 128)
                b.dma(lambda E, tsl=tsl: E.dma_start(out=xtile2, in_=din["x"][tsl, :]),
                      writes=["xtile"])
                for ht in range(HT):
                    pf = [ptfA[:, 0:128], ptfB][ht % 2]
                    pfk = "ptf%d" % (ht % 2)
                    sb_in = dA4a if ht < 4 else dA4b
                    off = (ht % 4) * TC + tbl * 128
                    b.op("t", lambda E, off=off, pf=pf, sb_in=sb_in: E.transpose(
                        pf[:, :], sb_in[:, off:off + 128], idf[:, :]),
                        reads=["sum.%d" % ht, "idf"], writes=[pfk])
                    b.op("v", lambda E, ht=ht, pf=pf: E.tensor_add(
                        out=ost[:, ht * 128:(ht + 1) * 128],
                        in0=pf[:, :],
                        in1=xtile2[:, ht * 128:(ht + 1) * 128]),
                        reads=[pfk, "xtile"], writes=["ost"])
                b.dma(lambda E, tsl=tsl: E.dma_start(out=dout[tsl, :], in_=ost[:, :]),
                      reads=["ost"], writes=["out.%d" % tb], q="g")

        for dsem in sorted(dysems) + sorted(dgsems) + sorted(dssems):
            if b.cnt[dsem]:
                b.raw_wait("y", dsem, b.cnt[dsem])

        @block.vector
        def _(E):
            for f in b.plan["v"]:
                f(E)

        @block.scalar
        def _(E):
            for f in b.plan["s"]:
                f(E)

        @block.tensor
        def _(E):
            for f in b.plan["t"]:
                f(E)

        @block.gpsimd
        def _(E):
            for f in b.plan["g"]:
                f(E)

        @block.sync
        def _(E):
            for f in b.plan["y"]:
                f(E)

    return nc


def kernel(**inputs):
    x = np.ascontiguousarray(np.asarray(inputs["L0_attn_x"], dtype=np.float32))
    names = {
        "Win": "L0_attn_Win", "bin": "L0_attn_bin", "Wx": "L0_attn_Wx",
        "bx": "L0_attn_bx", "Wdt": "L0_attn_Wdt", "bdt": "L0_attn_bdt",
        "A_log": "L0_attn_A_log", "Dp": "L0_attn_Dp", "Wout": "L0_attn_Wout",
        "bout": "L0_attn_bout", "clip_x": "L0_ffn_clip_x",
        "clip_down": "L0_ffn_clip_down", "w_gate": "L0_ffn_w_gate",
        "w_up": "L0_ffn_w_up", "w_down": "L0_ffn_w_down",
    }
    base = {k: np.ascontiguousarray(np.asarray(inputs[v], dtype=np.float32))
            for k, v in names.items()}
    base["bin"] = np.ascontiguousarray(base["bin"].reshape(32, 128).T)
    base["bdt"] = np.ascontiguousarray(base["bdt"].reshape(DT, 128).T)
    base["Dp"] = np.ascontiguousarray(base["Dp"].reshape(DT, 128).T)
    base["bout"] = np.ascontiguousarray(base["bout"].reshape(HT, 128).T)
    base["A_log"] = np.ascontiguousarray(
        base["A_log"].reshape(DT, 128, NS).transpose(1, 0, 2).reshape(128, DT * NS))
    base["Wx"] = np.ascontiguousarray(
        base["Wx"].reshape(DT, 128, 96).transpose(1, 0, 2).reshape(128, DT * 96))
    base["ident_b"] = np.eye(128).astype(ml_dtypes.bfloat16)
    base["ident_f"] = np.eye(128, dtype=np.float32)
    nc = build_program()
    in_maps = [dict(base, x=x[c]) for c in range(8)]
    res = run_bass_kernel_spmd(nc, in_maps, core_ids=list(range(8)))
    return np.stack([np.asarray(res.results[c]["out"]) for c in range(8)], axis=0)
